# version 13
# baseline (speedup 1.0000x reference)
import contextlib
import os
import numpy as np

_DBG = os.environ.get("KDBG", "")

import concourse.bass as bass
import concourse.mybir as mybir
from concourse.bass_utils import run_bass_kernel_spmd

F32 = mybir.dt.float32
BF16 = mybir.dt.bfloat16
AF = mybir.ActivationFunctionType
ALU = mybir.AluOpType
AX = mybir.AxisListType

NCORES = 8
C = 1024
G = 8
E = 8
DFF = 2048
EPS = 1e-5
BIG = 1e9


class Sched:

    ENGS = ("sync", "tensor", "vector", "scalar", "gpsimd")

    def __init__(self, nc, es):
        self.nc = nc
        self.es = es
        self.items = {e: [] for e in self.ENGS}
        self.cnt = {}
        self.sems = {}
        self.rot = {}

    def _sem(self, name):
        if name not in self.sems:
            self.sems[name] = self.es.enter_context(self.nc.semaphore(name))
            self.cnt[name] = 0
        return self.sems[name]

    def op(self, eng, fn, deps=(), inc=1, sem=None):
        if sem is None:
            sem = "e_" + eng
        self._sem(sem)
        waits = {}
        for d in deps:
            if d is None:
                continue
            s, v = d
            self._sem(s)
            if waits.get(s, 0) < v:
                waits[s] = v
        if inc:
            self.cnt[sem] += inc
        self.items[eng].append((fn, waits, sem if inc else None, inc))
        return (sem, self.cnt[sem]) if inc else None

    def tok(self, eng):
        sem = "e_" + eng
        self._sem(sem)
        return (sem, self.cnt[sem])

    ROT = ()

    def dma(self, eng, sem, out, in_, deps=()):
        def fn(q, out=out, in_=in_):
            return q.dma_start(out=out, in_=in_)
        if sem in self.ROT:
            n = self.rot.get(sem, 0)
            self.rot[sem] = n + 1
            sem = "%s_%d" % (sem, n // 14)
        self._sem(sem)
        deps = list(deps) + [(sem, self.cnt[sem])]
        return self.op(eng, fn, deps=deps, inc=16, sem=sem)

    def fence(self, eng, deps):
        return self.op(eng, lambda q: None, deps=deps, inc=0)

    def build(self, block):
        for eng in self.ENGS:
            items = self.items[eng]
            if not items:
                continue

            def body(q, items=items):
                vclock = {}
                for fn, waits, sem, inc in items:
                    for s in sorted(waits):
                        v = waits[s]
                        if vclock.get(s, 0) < v:
                            q.wait_ge(self.sems[s], v)
                            vclock[s] = v
                    inst = fn(q)
                    if inst is not None and sem is not None:
                        inst.then_inc(self.sems[sem], inc)

            getattr(block, eng)(body)


def build_program(tok=4096, tc=512):
    CH = tok // tc
    NT = tc // 128
    assert NT == 4 and tok % tc == 0

    nc = bass.Bass("TRN2", target_bir_lowering=False, debug=False)

    x_d = nc.dram_tensor("x", [tok, C], F32, kind="ExternalInput").ap()
    wiT_d = nc.dram_tensor("wiT", [C, C], F32, kind="ExternalInput").ap()
    wrT_d = nc.dram_tensor("wrT", [C, C], F32, kind="ExternalInput").ap()
    bi_d = nc.dram_tensor("bi", [128, G], F32, kind="ExternalInput").ap()
    br_d = nc.dram_tensor("br", [128, G], F32, kind="ExternalInput").ap()
    lam_d = nc.dram_tensor("lam", [128, 1], F32, kind="ExternalInput").ap()
    rw_d = nc.dram_tensor("rw", [C, E], F32, kind="ExternalInput").ap()
    up_d = nc.dram_tensor("up", [E, C, DFF], F32, kind="ExternalInput").ap()
    dn_d = nc.dram_tensor("dn", [E, C, C], F32, kind="ExternalInput").ap()
    csel_d = nc.dram_tensor("csel", [128, NCORES], F32, kind="ExternalInput").ap()
    out_d = nc.dram_tensor("out", [tok, C], F32, kind="ExternalOutput").ap()

    sp_at = nc.dram_tensor("sp_at", [CH, G, 128, tc], F32, kind="Internal").ap()
    sp_gx = nc.dram_tensor("sp_gx", [CH, G, 128, tc], F32, kind="Internal").ap()
    agin = nc.dram_tensor("agin", [2, 128, G], F32, kind="Internal")
    agout = nc.dram_tensor("agout", [NCORES, 2, 128, G], F32, kind="Internal")
    sc_u = nc.dram_tensor("sc_u", [E, 128, 2 * G * 2 * C], BF16, kind="Internal").ap()
    sc_d = nc.dram_tensor("sc_d", [E, 128, G * 2 * C], BF16, kind="Internal").ap()

    x_v = x_d.rearrange("(q u p) c -> q p u c", p=128, u=NT)
    out_v = out_d.rearrange("(q u p) c -> q p u c", p=128, u=NT)
    spat_v = sp_at.rearrange("q g p t -> q p g t")
    spgx_v = sp_gx.rearrange("q g p t -> q p g t")
    wiT_v = wiT_d.rearrange("(cg p) d -> p cg d", p=128)
    wrT_v = wrT_d.rearrange("(cg p) d -> p cg d", p=128)
    rw_v = rw_d.rearrange("(cg p) e -> p cg e", p=128)
    up_v = up_d.rearrange("e (cg p) (hh d1) -> e hh p cg d1", p=128, hh=2)
    dn_v = dn_d.rearrange("e (dg p) h -> e p dg h", p=128)
    agout_v = agout.ap().rearrange("r t p g -> p r t g")
    sc_u_p = sc_u.rearrange("e p (hh cg hl d1) -> e hh p cg hl d1", hh=2, cg=G, hl=2)
    sc_d_p = sc_d.rearrange("e p (dg hl h) -> e p dg hl h", dg=G, hl=2)

    es = contextlib.ExitStack()
    with es:
        bx = nc.alloc_sbuf_tensor("bx", [128, NT * C], F32)
        bxf = nc.alloc_sbuf_tensor("bxf", [128, G * tc], F32)
        bit = nc.alloc_sbuf_tensor("bit", [128, G * tc], F32)
        bat = nc.alloc_sbuf_tensor("bat", [128, G * tc], F32)
        bgx = nc.alloc_sbuf_tensor("bgx", [128, G * tc], F32)
        bacc = nc.alloc_sbuf_tensor("bacc", [128, NT * C], F32)
        wBh = nc.alloc_sbuf_tensor("wBh", [128, 2 * G * C], BF16)
        wBl = nc.alloc_sbuf_tensor("wBl", [128, 2 * G * C], BF16)
        wS = nc.alloc_sbuf_tensor("wS", [128, 4 * C], BF16)
        bxh = nc.alloc_sbuf_tensor("bxh", [128, G * tc], BF16)
        bxl = nc.alloc_sbuf_tensor("bxl", [128, G * tc], BF16)
        bhh = nc.alloc_sbuf_tensor("bhh", [128, G * tc], BF16)
        bhl = nc.alloc_sbuf_tensor("bhl", [128, G * tc], BF16)
        pstage = nc.alloc_sbuf_tensor("pstage", [128, C], F32)
        bident = nc.alloc_sbuf_tensor("bident", [128, 128], F32)
        bbi = nc.alloc_sbuf_tensor("bbi", [128, G], F32)
        bbr = nc.alloc_sbuf_tensor("bbr", [128, G], F32)
        brwh = nc.alloc_sbuf_tensor("brwh", [128, G * E], BF16)
        brwl = nc.alloc_sbuf_tensor("brwl", [128, G * E], BF16)
        blam = nc.alloc_sbuf_tensor("blam", [128, 1], F32)
        beps = nc.alloc_sbuf_tensor("beps", [128, 1], F32)
        bone = nc.alloc_sbuf_tensor("bone", [128, 1], F32)
        bcsel = nc.alloc_sbuf_tensor("bcsel", [128, NCORES], F32)
        bsilu = nc.alloc_sbuf_tensor("bsilu", [128, 512], F32)
        bhf = pstage
        bcarL = nc.alloc_sbuf_tensor("bcarL", [128, G], F32)
        bcarP = nc.alloc_sbuf_tensor("bcarP", [128, G], F32)
        bsP = nc.alloc_sbuf_tensor("bsP", [128, G], F32)
        btmpP = nc.alloc_sbuf_tensor("btmpP", [128, G], F32)
        bHrun = nc.alloc_sbuf_tensor("bHrun", [128, G], F32)
        bag = bx
        bcand_o, bhp_o, btmp8_o = 128, 192, 208
        bsum = nc.alloc_sbuf_tensor("bsum", [128, NT], F32)
        bssq = nc.alloc_sbuf_tensor("bssq", [128, NT], F32)
        bnm = nc.alloc_sbuf_tensor("bnm", [128, NT], F32)
        bm2t = nc.alloc_sbuf_tensor("bm2t", [128, NT], F32)
        bvar = nc.alloc_sbuf_tensor("bvar", [128, NT], F32)
        bstd = nc.alloc_sbuf_tensor("bstd", [128, NT], F32)
        brstd = nc.alloc_sbuf_tensor("brstd", [128, NT], F32)
        bnm1 = nc.alloc_sbuf_tensor("bnm1", [128, NT], F32)
        bnm2 = nc.alloc_sbuf_tensor("bnm2", [128, NT], F32)
        bgs = nc.alloc_sbuf_tensor("bgs", [128, NT], F32)
        bgr = nc.alloc_sbuf_tensor("bgr", [128, NT], F32)
        bge = nc.alloc_sbuf_tensor("bge", [128, NT * E], F32)
        bmsk = nc.alloc_sbuf_tensor("bmsk", [128, NT * E], F32)
        bsel = bmsk
        bgate = nc.alloc_sbuf_tensor("bgate", [128, NT * E], F32)

        pb = [nc.alloc_psum_tensor("pb%d" % i, [128, 512], F32) for i in range(8)]

        S = Sched(nc, es)
        block = es.enter_context(nc.Block())

        bx3 = bx.ap().rearrange("p (u c) -> p u c", c=C)
        bacc3 = bacc.ap().rearrange("p (u c) -> p u c", c=C)
        bacc_uc = bacc3
        bit_uc = bit.ap().rearrange("p (u c) -> p u c", c=C)
        bat3 = bat.ap().rearrange("p (g t) -> p g t", t=tc)
        bgx3 = bgx.ap().rearrange("p (g t) -> p g t", t=tc)
        bxf3 = bxf.ap().rearrange("p (g t) -> p g t", t=tc)
        bxh3 = bxh.ap().rearrange("p (g t) -> p g t", t=tc)
        bxl3 = bxl.ap().rearrange("p (g t) -> p g t", t=tc)
        bxf_uc = bxf.ap().rearrange("p (u c) -> p u c", c=C)
        wS2 = bxf.ap().bitcast(BF16)

        t_bi = S.dma("sync", "d_w", bbi[:, :], bi_d)
        t_br = S.dma("sync", "d_w", bbr[:, :], br_d)

        t_lam = S.dma("sync", "d_w", blam[:, :], lam_d)
        t_cs = S.dma("sync", "d_w", bcsel[:, :], csel_d)

        t_id0 = S.op("gpsimd", lambda q: q.memset(bident[:, :], 1.0))
        t_id = S.op("gpsimd", lambda q: q.affine_select(
            bident[:, :], bident[:, :], pattern=[[1, 128]],
            compare_op=ALU.is_equal, fill=0.0, channel_multiplier=-1),
            deps=[t_id0])
        t_eps = S.op("vector", lambda q: q.memset(beps[:, :], EPS))
        t_one = S.op("vector", lambda q: q.memset(bone[:, :], 1.0))

        t_c1 = S.op("scalar", lambda q: q.activation(blam[:, :], blam[:, :], AF.Exp),
                    deps=[t_lam])
        t_c2 = S.op("scalar", lambda q: q.activation(blam[:, :], blam[:, :], AF.Ln,
                                                     bias=bone[:, 0:1]),
                    deps=[t_c1, t_one])
        t_c = S.op("scalar", lambda q: q.mul(blam[:, :], blam[:, :], -8.0), deps=[t_c2])

        t_rw = S.dma("sync", "d_w",
                     pstage.ap()[:, :G * E].rearrange("p (g e) -> p g e", e=E),
                     rw_v)
        t_rwh = S.op("scalar", lambda q: q.activation(
            brwh[:, :], pstage[:, :G * E], AF.Identity), deps=[t_rw])
        t_rwl = S.op("vector", lambda q: q.tensor_tensor(
            brwl[:, :], pstage[:, :G * E], brwh[:, :], op=ALU.subtract),
            deps=[t_rwh])

        wBh_v = wBh.ap().rearrange("p (w cg d) -> p w cg d", w=2, cg=G)
        wBl_v = wBl.ap().rearrange("p (w cg d) -> p w cg d", w=2, cg=G)

        pc = [0]
        p_last = {}

        def split_common(src):
            i = pc[0]
            pc[0] += 1
            slot = bacc[:, (i % 4) * 1024:(i % 4) * 1024 + 1024]
            t_ld = S.dma("scalar", "d_ps%d" % (i % 4), slot, src,
                         deps=[p_last.get(("sub", i - 4))])
            p_last[("ld", i)] = t_ld
            return i, slot, t_ld

        def wsplit_res(src, dst_h, dst_l):
            i, slot, t_ld = split_common(src)
            t_cast = S.op("gpsimd", lambda q, d=dst_h, sl=slot: q.tensor_copy(
                d, sl), deps=[t_ld, p_last.get(("cast", i - 1))])
            p_last[("cast", i)] = t_cast
            t_sub = S.op("gpsimd", lambda q, d=dst_l, h=dst_h, sl=slot:
                         q.tensor_tensor(d, sl, h, op=ALU.subtract),
                         deps=[t_cast, p_last.get(("sub", i - 1))])
            p_last[("sub", i)] = t_sub
            return t_sub

        def wsplit_one(src, dst_hl):
            i, slot, t_ld = split_common(src)
            sbase = (i % 2) * 2048
            hs = wS[:, sbase:sbase + 1024]
            ls = wS[:, sbase + 1024:sbase + 2048]
            t_cast = S.op("gpsimd", lambda q, d=hs, sl=slot: q.tensor_copy(
                d, sl), deps=[t_ld, p_last.get(("cast", i - 1)),
                              p_last.get(("out", i - 2))])
            p_last[("cast", i)] = t_cast
            t_sub = S.op("gpsimd", lambda q, d=ls, h=hs, sl=slot:
                         q.tensor_tensor(d, sl, h, op=ALU.subtract),
                         deps=[t_cast, p_last.get(("sub", i - 1))])
            p_last[("sub", i)] = t_sub
            out = S.dma("gpsimd", "d_po%d" % (i % 2), dst_hl,
                        wS[:, sbase:sbase + 2048], deps=[t_sub])
            p_last[("out", i)] = out
            return out

        split_jobs = []
        for e_ in range(E):
            for hh in range(2):
                for cg_ in range(G):
                    split_jobs.append((e_, up_v[e_][hh][:, cg_, :],
                                       sc_u_p[e_][hh][:, cg_, :, :]))
            for dg_ in range(G):
                split_jobs.append((e_, dn_v[e_][:, dg_, :],
                                   sc_d_p[e_][:, dg_, :, :]))
        sj_pos = [0]
        t_split_done = [[] for _ in range(E)]

        def pump_splits(n):
            for _ in range(n):
                if sj_pos[0] >= len(split_jobs):
                    return
                e_, src, dhl = split_jobs[sj_pos[0]]
                sj_pos[0] += 1
                t_split_done[e_] = [wsplit_one(src, dhl)]

        t_wsp_all = []
        for wi, wv in ((0, wiT_v), (1, wrT_v)):
            for cg_ in range(G):
                t_wsp_all.append(wsplit_res(
                    wv[:, cg_, :],
                    wBh_v[:, wi, cg_, :],
                    wBl_v[:, wi, cg_, :]))


        def ln_block(src_ap, dump, norm_dst, deps_in, war_deps,
                     extra_norm_deps):
            t_d = None
            for u in range(NT):
                t_d = S.op("scalar", lambda q, u=u: q.activation(
                    dump, src_ap[:, u * C:(u + 1) * C], AF.Square,
                    accum_out=bssq[:, u:u + 1]),
                    deps=(deps_in + war_deps if u == 0 else [t_d]))
                t_d = S.op("scalar", lambda q, u=u: q.activation(
                    dump, src_ap[:, u * C:(u + 1) * C], AF.Identity,
                    accum_out=bsum[:, u:u + 1]), deps=[t_d])
            t_sq = t_sums = t_ssq = t_d
            t_nm = S.op("vector", lambda q: q.tensor_scalar(
                bnm[:, :], bsum[:, :], -1.0 / C, None, op0=ALU.mult), deps=[t_sums])
            t_m2 = S.op("vector", lambda q: q.tensor_tensor(
                bm2t[:, :], bnm[:, :], bnm[:, :], op=ALU.mult), deps=[t_nm])
            t_var = S.op("vector", lambda q: q.scalar_tensor_tensor(
                bvar[:, :], bssq[:, :], 1.0 / C, bm2t[:, :],
                op0=ALU.mult, op1=ALU.subtract), deps=[t_ssq, t_m2])
            t_std = S.op("scalar", lambda q: q.activation(
                bstd[:, :], bvar[:, :], AF.Sqrt, bias=beps[:, 0:1]),
                deps=[t_var, t_eps])
            t_rstd = S.op("vector", lambda q: q.reciprocal(
                brstd[:, :], bstd[:, :]), deps=[t_std])
            t_norm = []
            for u in range(NT):
                t_norm.append(S.op("vector", lambda q, u=u, s=src_ap,
                                   d=norm_dst: q.tensor_scalar(
                                       d[:, u * C:(u + 1) * C],
                                       s[:, u * C:(u + 1) * C],
                                       bnm[:, u:u + 1], brstd[:, u:u + 1],
                                       op0=ALU.add, op1=ALU.mult),
                                   deps=[t_rstd] + list(extra_norm_deps)))
            return t_norm, t_d, t_var

        t_evac = {}
        t_s_psum = [None]

        def tmfm_split_transpose(src, t_src_ready, dst_war, want_f32):
            t_cp = [[None] * 2 for _ in range(NT)]
            for u in range(NT):
                for hb in range(2):
                    bank = pb[hb]
                    for k in range(4):
                        g = hb * 4 + k
                        S.op("tensor", lambda q, u=u, g=g, k=k, bank=bank, s=src:
                             q.transpose(bank[:, k * 128:(k + 1) * 128],
                                         s[:, u * C + g * 128:u * C + (g + 1) * 128],
                                         bident[:, :]),
                             deps=([t_src_ready[u], t_id, t_evac.get(id(bank)),
                                    t_s_psum[0]]
                                   if k == 0 else []),
                             inc=(1 if k == 3 else 0))
                    t_tp = S.tok("tensor")
                    sl = (slice(None), slice(hb * 4, hb * 4 + 4),
                          slice(u * 128, (u + 1) * 128))
                    bank3 = bank.ap().rearrange("p (k t) -> p k t", t=128)
                    deps_c = [t_tp] + list(dst_war)
                    t_c32 = None
                    if want_f32:
                        t_c32 = S.op("vector", lambda q, d=bxf3[sl], b=bank3:
                                     q.tensor_copy(d, b), deps=deps_c)
                    t_ch = S.op("vector", lambda q, d=bxh3[sl], b=bank3:
                                q.tensor_copy(d, b), deps=deps_c)
                    if want_f32:
                        t_cl = S.op("vector", lambda q, d=bxl3[sl],
                                    s32=bxf3[sl], h=bxh3[sl]:
                                    q.tensor_tensor(
                                        d, s32, h, op=ALU.subtract),
                                    deps=[t_c32, t_ch])
                    else:
                        t_cl = S.op("vector", lambda q, d=bxl3[sl], b=bank3,
                                    h=bxh3[sl]: q.tensor_tensor(
                                        d, b, h, op=ALU.subtract),
                                    deps=[t_ch])
                    t_evac[id(bank)] = t_cl
                    t_cp[u][hb] = t_cl
            return t_cp

        t_bx_free = None
        t_bit_r_free = None
        t_bsplit_war = [t_rwl]
        t_bgx_r_free = None
        t_sp_at_prev = None
        t_sp_gx_prev = None
        t_carL = [None] * G
        t_eP_prev = None
        t_carP_all = None
        t_scr_free = None

        t_lx_pend = S.dma("sync", "d_x", bx3, x_v[0], deps=[t_bx_free])
        t_norm_pend, t_sqd_pend, t_var_prev = ln_block(
            bx.ap(), bxh.ap()[:, :C], bx.ap(),
            deps_in=[t_lx_pend], war_deps=[], extra_norm_deps=[])
        for q in range(CH):
            t_lx = t_lx_pend
            t_norm = t_norm_pend
            t_fmcp = tmfm_split_transpose(
                bx.ap(), t_norm, t_bsplit_war + [t_sqd_pend], True)
            t_fm_all = [t_fmcp[u][hb] for u in range(NT) for hb in range(2)]

            t_sig = {0: [None] * G, 1: [None] * G}
            for wi in range(2):
                for dg in range(G):
                    bank = pb[3 + (dg % 2)]
                    passes = (((wBh, bxh), (wBh, bxl), (wBl, bxh))
                              if "no3p" not in _DBG else ((wBh, bxh),))
                    npss = len(passes) - 1
                    for cg in range(G):
                        for ps, (wb, xb) in enumerate(passes):
                            deps = []
                            if cg == 0 and ps == 0:
                                deps = t_fm_all + t_wsp_all + \
                                    [t_evac.get(id(bank))]
                            S.op("tensor", lambda q_, wi=wi, dg=dg, cg=cg,
                                 ps=ps, bank=bank, wb=wb, xb=xb:
                                 q_.matmul(
                                     bank[:, :tc],
                                     wb[:, wi * G * C + cg * C + dg * 128:
                                        wi * G * C + cg * C + (dg + 1) * 128],
                                     xb[:, cg * tc:(cg + 1) * tc],
                                     start=(cg == 0 and ps == 0),
                                     stop=(cg == G - 1 and ps == npss)),
                                 deps=deps,
                                 inc=(1 if (cg == G - 1 and ps == npss) else 0))
                    t_mm = S.tok("tensor")
                    if wi == 0:
                        t_s = S.op("scalar", lambda q_, dg=dg, bank=bank:
                                   q_.activation(
                                       bit[:, dg * tc:(dg + 1) * tc],
                                       bank[:, :tc], AF.Sigmoid,
                                       bias=bbi[:, dg:dg + 1]),
                                   deps=[t_mm, t_bi, t_bit_r_free])
                    else:
                        t_s = S.op("scalar", lambda q_, dg=dg, bank=bank:
                                   q_.activation(
                                       bat[:, dg * tc:(dg + 1) * tc],
                                       bank[:, :tc], AF.Sigmoid,
                                       bias=bbr[:, dg:dg + 1]),
                                   deps=[t_mm, t_br, t_sp_at_prev])
                    t_sig[wi][dg] = t_s
                    t_evac[id(bank)] = t_s
                    t_s_psum[0] = t_s
                    pump_splits(1)
            t_pa_tok = S.tok("tensor")
            if q + 1 < CH:
                t_lx_pend = S.dma("sync", "d_x", bx3, x_v[q + 1],
                                  deps=[t_fm_all[-1]])
                t_norm_pend, t_sqd_pend, t_var_prev = ln_block(
                    bx.ap(), bxh.ap()[:, :C], bx.ap(),
                    deps_in=[t_lx_pend, t_var_prev], war_deps=[t_pa_tok],
                    extra_norm_deps=[])
            t_red = [None] * G
            for dg in range(G):
                t_red[dg] = S.op("vector", lambda q_, dg=dg: q_.reduce_sum(
                    bsP[:, dg:dg + 1], bat[:, dg * tc:(dg + 1) * tc],
                    axis=AX.X), deps=[t_sig[1][dg], t_eP_prev])
            t_at = []
            for dg in range(G):
                t_at.append(S.op("scalar", lambda q_, dg=dg: q_.activation(
                    bat[:, dg * tc:(dg + 1) * tc], bat[:, dg * tc:(dg + 1) * tc],
                    AF.Exp, scale=blam[:, 0:1]),
                    deps=[t_sig[1][dg], t_c, t_red[dg]]))
            t_eP = S.op("scalar", lambda q_: q_.activation(
                btmpP[:, :], bsP[:, :], AF.Exp, scale=blam[:, 0:1]),
                deps=t_red + [t_carP_all])
            t_eP_prev = t_eP
            if q == 0:
                t_carP_all = S.op("vector", lambda q_: q_.tensor_copy(
                    bcarP[:, :], btmpP[:, :]), deps=[t_eP])
            else:
                t_carP_all = S.op("vector", lambda q_: q_.tensor_tensor(
                    bcarP[:, :], bcarP[:, :], btmpP[:, :], op=ALU.mult),
                    deps=[t_eP, t_carP_all])

            t_ixg = [None] * G
            t_ngg = [None] * G
            for dg in range(G):
                t_ixg[dg] = S.op("vector", lambda q_, dg=dg: q_.tensor_tensor(
                    bit[:, dg * tc:(dg + 1) * tc], bit[:, dg * tc:(dg + 1) * tc],
                    bxf[:, dg * tc:(dg + 1) * tc], op=ALU.mult),
                    deps=[t_sig[0][dg]])
                t_ngg[dg] = S.op("vector", lambda q_, dg=dg: q_.scalar_tensor_tensor(
                    bgx[:, dg * tc:(dg + 1) * tc], bat[:, dg * tc:(dg + 1) * tc],
                    -1.0, bat[:, dg * tc:(dg + 1) * tc],
                    op0=ALU.mult, op1=ALU.mult),
                    deps=[t_at[dg], t_sp_gx_prev, t_bgx_r_free])
            t_cfg = S.op("scalar", lambda q_: q_.activation(
                bgx[:, :], bgx[:, :], AF.Sqrt, bias=bone[:, 0:1]),
                deps=t_ngg + [t_one])
            pump_splits(8)
            t_gxg = [None] * G
            for g in range(G):
                t_gxg[g] = S.op("vector", lambda q_, g=g: q_.tensor_tensor(
                    bgx[:, g * tc:(g + 1) * tc], bgx[:, g * tc:(g + 1) * tc],
                    bit[:, g * tc:(g + 1) * tc], op=ALU.mult),
                    deps=[t_cfg, t_ixg[g]])
                initL = 0.0 if q == 0 else bcarL[:, g:g + 1]
                t_ls = S.op("vector", lambda q_, g=g, initL=initL:
                            q_.tensor_tensor_scan(
                                bsilu[:, :], bat[:, g * tc:(g + 1) * tc],
                                bgx[:, g * tc:(g + 1) * tc], initL,
                                op0=ALU.mult, op1=ALU.add),
                            deps=[t_gxg[g], t_carL[g], t_scr_free])
                t_carL[g] = S.op("vector", lambda q_, g=g: q_.tensor_copy(
                    bcarL[:, g:g + 1], bsilu[:, tc - 1:tc]), deps=[t_ls])
                t_scr_free = t_carL[g]

            t_sp_at_prev = S.dma("sync", "d_spat", spat_v[q], bat3, deps=t_at)
            t_sp_gx_prev = S.dma("sync", "d_spgx", spgx_v[q], bgx3,
                                 deps=list(t_gxg))
            t_bit_r_free = t_gxg[G - 1]
            t_bsplit_war = [S.tok("tensor")]
            t_bgx_r_free = t_carL[G - 1]
            t_bx_free = t_fm_all[-1]



        t_e1 = S.dma("sync", "d_ag", agin.ap()[0], bcarP[:, :],
                     deps=[t_carP_all])
        t_e2 = S.dma("sync", "d_ag", agin.ap()[1], bcarL[:, :],
                     deps=[t_carL[g] for g in range(G)])
        t_cc = S.op("gpsimd", lambda q_: q_.collective_compute(
            "AllGather", ALU.bypass, replica_groups=[list(range(NCORES))],
            ins=[agin.ap().opt()], outs=[agout.ap().opt()]),
            deps=[t_e1, t_e2], sem="s_cc")
        t_bag = S.dma("sync", "d_bag",
                      bag.ap()[:, :NCORES * 2 * G].rearrange(
                          "p (r t g) -> p r t g", t=2, g=G),
                      agout_v, deps=[t_cc, t_bx_free])

        def cnd(i):
            return bag[:, bcand_o + i * G:bcand_o + (i + 1) * G]

        t_cand = [None] * NCORES
        t_cand[0] = S.op("vector", lambda q_: q_.memset(cnd(0), 0.0),
                         deps=[t_bag])
        t_cand[4] = S.op("vector", lambda q_: q_.memset(cnd(4), 0.0),
                         deps=[t_bag])
        t_btmp_r = None
        for b in range(2):
            for k in range(1, 4):
                i = 4 * b + k
                r = 4 * b + k - 1
                tm = S.op("vector", lambda q_, i=i, r=r: q_.tensor_tensor(
                    bag[:, btmp8_o:btmp8_o + G], bag[:, r * 16:r * 16 + G],
                    cnd(i - 1), op=ALU.mult),
                    deps=[t_bag, t_cand[i - 1], t_btmp_r])
                t_cand[i] = S.op("vector", lambda q_, i=i, r=r: q_.tensor_tensor(
                    cnd(i), bag[:, r * 16 + G:r * 16 + 2 * G],
                    bag[:, btmp8_o:btmp8_o + G], op=ALU.add), deps=[tm])
                t_btmp_r = t_cand[i]
        t_h = S.op("vector", lambda q_: q_.tensor_scalar(
            bag[:, bhp_o:bhp_o + G], cnd(0), bcsel[:, 0:1], None, op0=ALU.mult),
            deps=[t_cand[0], t_cs])
        for i in range(1, NCORES):
            src = ((i + 1) % 2) * G
            dst = (i % 2) * G
            t_h = S.op("vector", lambda q_, i=i, src=src, dst=dst:
                       q_.scalar_tensor_tensor(
                           bag[:, bhp_o + dst:bhp_o + dst + G], cnd(i),
                           bcsel[:, i:i + 1], bag[:, bhp_o + src:bhp_o + src + G],
                           op0=ALU.mult, op1=ALU.add),
                       deps=[t_cand[i], t_h])
        t_hrun = S.op("vector", lambda q_: q_.tensor_copy(
            bHrun[:, :], bag[:, bhp_o + G:bhp_o + 2 * G]), deps=[t_h])

        pump_splits(len(split_jobs))

        t_pa_pe = S.tok("tensor")
        t_lat_free = t_sp_at_prev
        t_lgx_free = t_sp_gx_prev
        t_bx_free_b = t_bx_free
        t_bit_free = t_bit_r_free
        t_ws2_war = [t_bit_r_free]
        t_hup = [t_hrun] * G
        t_out_prev = p_last.get(("sub", pc[0] - 1))
        t_up_used = {0: t_pa_pe, 1: t_pa_pe}
        t_dn_war = {0: [p_last.get(("out", pc[0] - 1)),
                        p_last.get(("out", pc[0] - 2))],
                    1: [t_bit_r_free]}
        t_hid_prev = t_scr_free
        t_bhid_war = None
        t_bhf_war = t_rwl
        sp_at_all = ("d_spat", 16 * CH)
        sp_gx_all = ("d_spgx", 16 * CH)

        for q in range(CH):
            t_lat = S.dma("scalar", "d_lat", bat3, spat_v[q],
                          deps=[sp_at_all, t_lat_free])
            t_lgx = S.dma("scalar", "d_lgx", bgx3, spgx_v[q],
                          deps=[sp_gx_all, t_lgx_free])
            t_lx = S.dma("sync", "d_x", bx3, x_v[q],
                         deps=[t_bx_free_b, t_hrun if q == 0 else None])

            t_scan = [None] * G
            for g in range(G):
                t_scan[g] = S.op("vector", lambda q_, g=g: q_.tensor_tensor_scan(
                    bit[:, g * tc:(g + 1) * tc], bat[:, g * tc:(g + 1) * tc],
                    bgx[:, g * tc:(g + 1) * tc], bHrun[:, g:g + 1],
                    op0=ALU.mult, op1=ALU.add),
                    deps=[t_lat, t_lgx, t_hup[g], t_bit_free])
                t_hup[g] = S.op("vector", lambda q_, g=g: q_.tensor_copy(
                    bHrun[:, g:g + 1], bit[:, g * tc + tc - 1:g * tc + tc]),
                    deps=[t_scan[g]])
            t_lat_free = t_scan[G - 1]
            t_lgx_free = t_scan[G - 1]

            t_res = []
            for u in range(NT):
                for hb in range(2):
                    bank = pb[hb]
                    for k in range(4):
                        g = hb * 4 + k
                        S.op("tensor", lambda q_, u=u, g=g, k=k, bank=bank:
                             q_.transpose(bank[:, k * 128:(k + 1) * 128],
                                          bit[:, g * tc + u * 128:g * tc + (u + 1) * 128],
                                          bident[:, :]),
                             deps=([t_scan[hb * 4 + kk] for kk in range(4)]
                                   + [t_id, t_evac.get(id(bank)), t_s_psum[0]]
                                   if k == 0 else []),
                             inc=(1 if k == 3 else 0))
                    t_tp = S.tok("tensor")
                    t_r = S.op("vector", lambda q_, u=u, hb=hb, bank=bank:
                               q_.tensor_tensor(
                                   bacc[:, u * C + hb * 512:u * C + (hb + 1) * 512],
                                   bank[:, :512],
                                   bx[:, u * C + hb * 512:u * C + (hb + 1) * 512],
                                   op=ALU.add),
                               deps=[t_tp, t_lx, t_out_prev])
                    t_evac[id(bank)] = t_r
                    t_res.append(t_r)
            t_bx_free_b = t_res[-1]

            t_norm, t_sqd2, t_var_prev = ln_block(
                bacc.ap(), bgx.ap()[:, :C], bit.ap(),
                deps_in=t_res + [t_var_prev], war_deps=[t_scan[G - 1]],
                extra_norm_deps=[t_hup[G - 1]])
            t_lgx_free = t_sqd2

            t_x2f = tmfm_split_transpose(bit.ap(), t_norm, t_bsplit_war, False)
            t_x2f_all = [t_x2f[u][hb] for u in range(NT) for hb in range(2)]

            t_gate = [None] * NT
            prbank = pb[2]
            for u in range(NT if "norouter" not in _DBG else 0):
                for cg in range(G):
                    for ps, (xb, rb) in enumerate(
                            ((bxh, brwh), (bxl, brwh), (bxh, brwl))):
                        deps = []
                        if cg == 0 and ps == 0:
                            deps = [t_x2f[u][0], t_x2f[u][1], t_rwl,
                                    t_evac.get(id(prbank))]
                        S.op("tensor", lambda q_, u=u, cg=cg, ps=ps, xb=xb, rb=rb:
                             q_.matmul(
                                 prbank[:, :E],
                                 xb[:, cg * tc + u * 128:cg * tc + (u + 1) * 128],
                                 rb[:, cg * E:(cg + 1) * E],
                                 start=(cg == 0 and ps == 0),
                                 stop=(cg == G - 1 and ps == 2)),
                             deps=deps,
                             inc=(1 if (cg == G - 1 and ps == 2) else 0))
                t_rmm = S.tok("tensor")
                t_m1 = S.op("vector", lambda q_, u=u: q_.reduce_max(
                    bnm1[:, u:u + 1], prbank[:, :E], axis=AX.X, negate=True),
                    deps=[t_rmm])
                t_ex = S.op("scalar", lambda q_, u=u: q_.activation(
                    bge[:, u * E:(u + 1) * E], prbank[:, :E], AF.Exp,
                    bias=bnm1[:, u:u + 1]), deps=[t_m1, t_x2f_all[-1]])
                t_s_psum[0] = t_ex
                t_mk = S.op("vector", lambda q_, u=u: q_.tensor_scalar(
                    bmsk[:, u * E:(u + 1) * E], prbank[:, :E],
                    bnm1[:, u:u + 1], 0.0, op0=ALU.add, op1=ALU.is_ge),
                    deps=[t_m1])
                t_mk2 = S.op("vector", lambda q_, u=u: q_.scalar_tensor_tensor(
                    bmsk[:, u * E:(u + 1) * E], bmsk[:, u * E:(u + 1) * E],
                    -BIG, prbank[:, :E], op0=ALU.mult, op1=ALU.add),
                    deps=[t_mk])
                t_m2r = S.op("vector", lambda q_, u=u: q_.reduce_max(
                    bnm2[:, u:u + 1], bmsk[:, u * E:(u + 1) * E], axis=AX.X,
                    negate=True), deps=[t_mk2])
                t_se = S.op("vector", lambda q_, u=u: q_.tensor_scalar(
                    bsel[:, u * E:(u + 1) * E], prbank[:, :E],
                    bnm2[:, u:u + 1], 0.0, op0=ALU.add, op1=ALU.is_ge),
                    deps=[t_m2r])
                t_g0 = S.op("vector", lambda q_, u=u: q_.tensor_tensor(
                    bge[:, u * E:(u + 1) * E], bge[:, u * E:(u + 1) * E],
                    bsel[:, u * E:(u + 1) * E], op=ALU.mult), deps=[t_se, t_ex])
                t_evac[id(prbank)] = t_g0
                t_gs = S.op("vector", lambda q_, u=u: q_.reduce_sum(
                    bgs[:, u:u + 1], bge[:, u * E:(u + 1) * E], axis=AX.X),
                    deps=[t_g0])
                t_gr = S.op("vector", lambda q_, u=u: q_.reciprocal(
                    bgr[:, u:u + 1], bgs[:, u:u + 1]), deps=[t_gs])
                t_gate[u] = S.op("vector", lambda q_, u=u: q_.tensor_scalar(
                    bgate[:, u * E:(u + 1) * E], bge[:, u * E:(u + 1) * E],
                    bgr[:, u:u + 1], None, op0=ALU.mult), deps=[t_gr])

            t_acc_last = t_res
            for e in range(E if "noffn" not in _DBG else 0):
                sd = t_split_done[e]
                sc_u_e = sc_u[e].rearrange("p (pp hl d1) -> p hl pp d1", hl=2, d1=1024)
                t_upA = S.dma("sync", "d_uph",
                              wBh.ap().rearrange("p (pp d1) -> p pp d1", d1=C),
                              sc_u_e[:, 0], deps=[t_up_used[0]] + sd)
                t_upB = S.dma("sync", "d_upl",
                              wBl.ap().rearrange("p (pp d1) -> p pp d1", d1=C),
                              sc_u_e[:, 1], deps=[t_up_used[1]] + sd)
                t_hid = [None] * G
                for dk in range(G):
                    for which, bank in ((0, pb[3]), (1, pb[4])):
                        for cg in range(G):
                            for ps, (wb, xb) in enumerate(
                                    ((wBh, bxh), (wBh, bxl), (wBl, bxh))):
                                deps = []
                                if cg == 0 and ps == 0:
                                    deps = t_x2f_all + [t_upA, t_upB,
                                                       t_evac.get(id(bank))]
                                S.op("tensor", lambda q_, dk=dk, cg=cg, ps=ps,
                                     which=which, bank=bank, wb=wb, xb=xb:
                                     q_.matmul(
                                         bank[:, :512],
                                         wb[:, which * G * C + cg * C + dk * 128:
                                            which * G * C + cg * C + (dk + 1) * 128],
                                         xb[:, cg * tc:(cg + 1) * tc],
                                         start=(cg == 0 and ps == 0),
                                         stop=(cg == G - 1 and ps == 2)),
                                     deps=deps,
                                     inc=(1 if (cg == G - 1 and ps == 2) else 0))
                        if which == 0:
                            t_h1 = S.tok("tensor")
                        else:
                            t_h2 = S.tok("tensor")
                    t_si = S.op("scalar", lambda q_: q_.activation(
                        bsilu[:, :], pb[3][:, :512], AF.Silu),
                        deps=[t_h1, t_hid_prev])
                    t_evac[id(pb[3])] = t_si
                    t_s_psum[0] = t_si
                    t_tt = S.op("vector", lambda q_: q_.tensor_tensor(
                        bhf[:, :512], bsilu[:, :], pb[4][:, :512], op=ALU.mult),
                        deps=[t_h2, t_si, t_bhf_war, t_scan[G - 1]])
                    t_evac[id(pb[4])] = t_tt
                    t_hh2 = S.op("scalar", lambda q_, dk=dk: q_.activation(
                        bhh[:, dk * 512:(dk + 1) * 512], bhf[:, :512],
                        AF.Identity), deps=[t_tt, t_bhid_war])
                    t_hl2 = S.op("vector", lambda q_, dk=dk: q_.tensor_tensor(
                        bhl[:, dk * 512:(dk + 1) * 512], bhf[:, :512],
                        bhh[:, dk * 512:(dk + 1) * 512], op=ALU.subtract),
                        deps=[t_hh2])
                    t_bhf_war = t_hl2
                    t_hid[dk] = t_hl2
                    t_hid_prev = t_tt
                    t_bhf_war = t_hl2

                t_up_done = S.tok("tensor")

                t_dmm = {}
                t_dld = {}
                for r in range(4):
                    rb = wS.ap() if r % 2 == 0 else wS2[:, :4096]
                    t_dh = S.dma("sync", "d_dnh",
                                 rb[:, :2048].rearrange(
                                     "p (k d) -> p k d", k=2),
                                 sc_d_p[e][:, 2 * r:2 * r + 2, 0, :],
                                 deps=t_dn_war[r % 2])
                    t_dl = S.dma("sync", "d_dnl",
                                 rb[:, 2048:4096].rearrange(
                                     "p (k d) -> p k d", k=2),
                                 sc_d_p[e][:, 2 * r:2 * r + 2, 1, :],
                                 deps=t_dn_war[r % 2])
                    t_dld[r] = [t_dh, t_dl]
                    for u in range(NT):
                        for hc in range(2):
                            bank = pb[u * 2 + hc]
                            for dgl in range(2):
                                dg = 2 * r + dgl
                                for ps, (hb_, dof) in enumerate(
                                        ((bhh, 0), (bhh, 2048), (bhl, 0))):
                                    deps = []
                                    if r == 0 and dgl == 0 and ps == 0:
                                        deps = [t_dh, t_dl,
                                                t_evac.get(id(bank))] + t_hid
                                    elif dgl == 0 and ps == 0 and u == 0 and hc == 0:
                                        deps = [t_dh, t_dl]
                                    S.op("tensor", lambda q_, u=u, hc=hc, dg=dg,
                                         dgl=dgl, r=r, ps=ps, bank=bank,
                                         hb_=hb_, dof=dof, rb=rb:
                                         q_.matmul(
                                             bank[:, :512],
                                             hb_[:, dg * 512 + u * 128:
                                                 dg * 512 + (u + 1) * 128],
                                             rb[:, dof + dgl * C + hc * 512:
                                                dof + dgl * C + (hc + 1) * 512],
                                             start=(r == 0 and dgl == 0 and ps == 0),
                                             stop=(r == 3 and dgl == 1 and ps == 2)),
                                         deps=deps,
                                         inc=(1 if (dgl == 1 and ps == 2
                                                    and ((u == NT - 1 and hc == 1)
                                                         or r == 3))
                                              else 0))
                                if r == 3 and dgl == 1:
                                    t_dmm[(u, hc)] = S.tok("tensor")
                    t_dn_war[r % 2] = [S.tok("tensor")]
                t_acc_new = [None] * 8
                for u, hc in ((0, 0), (0, 1), (1, 0), (1, 1), (2, 0), (2, 1),
                              (3, 0), (3, 1)):
                    bank = pb[u * 2 + hc]
                    t_a = S.op("vector", lambda q_, u=u, hc=hc, e=e,
                               bank=bank: q_.scalar_tensor_tensor(
                                   bacc[:, u * C + hc * 512:u * C + (hc + 1) * 512],
                                   bank[:, :512],
                                   bgate[:, u * E + e:u * E + e + 1],
                                   bacc[:, u * C + hc * 512:u * C + (hc + 1) * 512],
                                   op0=ALU.mult, op1=ALU.add),
                               deps=[t_dmm[(u, hc)], t_gate[u],
                                     t_acc_last[u * 2 + hc]])
                    t_evac[id(bank)] = t_a
                    t_acc_new[u * 2 + hc] = t_a
                t_acc_last = t_acc_new
                t_pe_now = S.tok("tensor")
                t_up_used = {0: t_up_done, 1: t_up_done}
                t_bhid_war = t_pe_now
                t_bsplit_war = [t_pe_now]

            t_out_prev = S.dma("sync", "d_out", out_v[q], bacc3,
                               deps=t_acc_last)
            t_lat_free = t_bhf_war
            t_bit_free = t_x2f_all[-1]

        S.fence("sync", [("d_out", 16 * CH), t_out_prev])
        S.build(block)
    return nc


def _stage(inputs, tok):
    x = np.ascontiguousarray(np.asarray(inputs["L0_attn_x"], np.float32)
                             .reshape(-1, C))
    wiT = np.ascontiguousarray(np.asarray(inputs["L0_attn_Wi"], np.float32).T)
    wrT = np.ascontiguousarray(np.asarray(inputs["L0_attn_Wr"], np.float32).T)
    bi = np.ascontiguousarray(np.asarray(inputs["L0_attn_bi"], np.float32)
                              .reshape(G, 128).T)
    br = np.ascontiguousarray(np.asarray(inputs["L0_attn_br"], np.float32)
                              .reshape(G, 128).T)
    lam = np.ascontiguousarray(
        np.tile(np.asarray(inputs["L0_attn_cap_lambda"], np.float32)
                .reshape(1, 1), (128, 1)))
    rw = np.ascontiguousarray(np.asarray(inputs["L0_ffn_router_w"], np.float32).T)
    up = np.ascontiguousarray(np.asarray(inputs["L0_ffn_up_proj"], np.float32))
    dn = np.ascontiguousarray(np.asarray(inputs["L0_ffn_down_proj"], np.float32))
    in_maps = []
    for c in range(NCORES):
        csel = np.ascontiguousarray(
            np.tile(np.eye(NCORES, dtype=np.float32)[c], (128, 1)))
        in_maps.append({
            "x": np.ascontiguousarray(x[c * tok:(c + 1) * tok]),
            "wiT": wiT, "wrT": wrT, "bi": bi, "br": br, "lam": lam,
            "rw": rw, "up": up, "dn": dn, "csel": csel,
        })
    return in_maps


def kernel(**inputs):
    B, T, _ = inputs["L0_attn_x"].shape
    tok = B * T // NCORES
    in_maps = _stage(inputs, tok)
    nc = build_program(tok=tok, tc=512)
    res = run_bass_kernel_spmd(nc, in_maps, core_ids=list(range(NCORES)))
    out = np.concatenate([np.asarray(r["out"]) for r in res.results], axis=0)
    return out.reshape(B, T, C).astype(np.float32)


# version 14
# speedup vs baseline: 1.0109x; 1.0109x over previous
import contextlib
import os
import numpy as np

_DBG = os.environ.get("KDBG", "")

import concourse.bass as bass
import concourse.mybir as mybir
from concourse.bass_utils import run_bass_kernel_spmd

F32 = mybir.dt.float32
BF16 = mybir.dt.bfloat16
AF = mybir.ActivationFunctionType
ALU = mybir.AluOpType
AX = mybir.AxisListType

NCORES = 8
C = 1024
G = 8
E = 8
DFF = 2048
EPS = 1e-5
BIG = 1e9


class Sched:

    ENGS = ("sync", "tensor", "vector", "scalar", "gpsimd")

    def __init__(self, nc, es):
        self.nc = nc
        self.es = es
        self.items = {e: [] for e in self.ENGS}
        self.cnt = {}
        self.sems = {}
        self.rot = {}

    def _sem(self, name):
        if name not in self.sems:
            self.sems[name] = self.es.enter_context(self.nc.semaphore(name))
            self.cnt[name] = 0
        return self.sems[name]

    def op(self, eng, fn, deps=(), inc=1, sem=None):
        if sem is None:
            sem = "e_" + eng
        self._sem(sem)
        waits = {}
        for d in deps:
            if d is None:
                continue
            s, v = d
            self._sem(s)
            if waits.get(s, 0) < v:
                waits[s] = v
        if inc:
            self.cnt[sem] += inc
        self.items[eng].append((fn, waits, sem if inc else None, inc))
        return (sem, self.cnt[sem]) if inc else None

    def tok(self, eng):
        sem = "e_" + eng
        self._sem(sem)
        return (sem, self.cnt[sem])

    ROT = ()

    def dma(self, eng, sem, out, in_, deps=()):
        def fn(q, out=out, in_=in_):
            return q.dma_start(out=out, in_=in_)
        if sem in self.ROT:
            n = self.rot.get(sem, 0)
            self.rot[sem] = n + 1
            sem = "%s_%d" % (sem, n // 14)
        self._sem(sem)
        deps = list(deps) + [(sem, self.cnt[sem])]
        return self.op(eng, fn, deps=deps, inc=16, sem=sem)

    def fence(self, eng, deps):
        return self.op(eng, lambda q: None, deps=deps, inc=0)

    def build(self, block):
        for eng in self.ENGS:
            items = self.items[eng]
            if not items:
                continue

            def body(q, items=items):
                vclock = {}
                for fn, waits, sem, inc in items:
                    for s in sorted(waits):
                        v = waits[s]
                        if vclock.get(s, 0) < v:
                            q.wait_ge(self.sems[s], v)
                            vclock[s] = v
                    inst = fn(q)
                    if inst is not None and sem is not None:
                        inst.then_inc(self.sems[sem], inc)

            getattr(block, eng)(body)


def build_program(tok=4096, tc=512):
    CH = tok // tc
    NT = tc // 128
    assert NT == 4 and tok % tc == 0

    nc = bass.Bass("TRN2", target_bir_lowering=False, debug=False)

    x_d = nc.dram_tensor("x", [tok, C], F32, kind="ExternalInput").ap()
    wiT_d = nc.dram_tensor("wiT", [C, C], F32, kind="ExternalInput").ap()
    wrT_d = nc.dram_tensor("wrT", [C, C], F32, kind="ExternalInput").ap()
    bi_d = nc.dram_tensor("bi", [128, G], F32, kind="ExternalInput").ap()
    br_d = nc.dram_tensor("br", [128, G], F32, kind="ExternalInput").ap()
    lam_d = nc.dram_tensor("lam", [128, 1], F32, kind="ExternalInput").ap()
    rw_d = nc.dram_tensor("rw", [C, E], F32, kind="ExternalInput").ap()
    up_d = nc.dram_tensor("up", [E, C, DFF], F32, kind="ExternalInput").ap()
    dn_d = nc.dram_tensor("dn", [E, C, C], F32, kind="ExternalInput").ap()
    csel_d = nc.dram_tensor("csel", [128, NCORES], F32, kind="ExternalInput").ap()
    out_d = nc.dram_tensor("out", [tok, C], F32, kind="ExternalOutput").ap()

    sp_at = nc.dram_tensor("sp_at", [CH, G, 128, tc], F32, kind="Internal").ap()
    sp_gx = nc.dram_tensor("sp_gx", [CH, G, 128, tc], F32, kind="Internal").ap()
    agin = nc.dram_tensor("agin", [2, 128, G], F32, kind="Internal")
    agout = nc.dram_tensor("agout", [NCORES, 2, 128, G], F32, kind="Internal")
    sc_u = nc.dram_tensor("sc_u", [E, 128, 2 * G * 2 * C], BF16, kind="Internal").ap()
    sc_d = nc.dram_tensor("sc_d", [E, 128, G * 2 * C], BF16, kind="Internal").ap()

    x_v = x_d.rearrange("(q u p) c -> q p u c", p=128, u=NT)
    out_v = out_d.rearrange("(q u p) c -> q p u c", p=128, u=NT)
    spat_v = sp_at.rearrange("q g p t -> q p g t")
    spgx_v = sp_gx.rearrange("q g p t -> q p g t")
    wiT_v = wiT_d.rearrange("(cg p) d -> p cg d", p=128)
    wrT_v = wrT_d.rearrange("(cg p) d -> p cg d", p=128)
    rw_v = rw_d.rearrange("(cg p) e -> p cg e", p=128)
    up_v = up_d.rearrange("e (cg p) (hh d1) -> e hh p cg d1", p=128, hh=2)
    dn_v = dn_d.rearrange("e (dg p) h -> e p dg h", p=128)
    agout_v = agout.ap().rearrange("r t p g -> p r t g")
    sc_u_p = sc_u.rearrange("e p (hh cg hl d1) -> e hh p cg hl d1", hh=2, cg=G, hl=2)
    sc_d_p = sc_d.rearrange("e p (dg hl h) -> e p dg hl h", dg=G, hl=2)

    es = contextlib.ExitStack()
    with es:
        bx = nc.alloc_sbuf_tensor("bx", [128, NT * C], F32)
        bxf = nc.alloc_sbuf_tensor("bxf", [128, G * tc], F32)
        bit = nc.alloc_sbuf_tensor("bit", [128, G * tc], F32)
        bat = nc.alloc_sbuf_tensor("bat", [128, G * tc], F32)
        bgx = nc.alloc_sbuf_tensor("bgx", [128, G * tc], F32)
        bacc = nc.alloc_sbuf_tensor("bacc", [128, NT * C], F32)
        wBh = nc.alloc_sbuf_tensor("wBh", [128, 2 * G * C], BF16)
        wBl = nc.alloc_sbuf_tensor("wBl", [128, 2 * G * C], BF16)
        wS = nc.alloc_sbuf_tensor("wS", [128, 4 * C], BF16)
        bxh = nc.alloc_sbuf_tensor("bxh", [128, G * tc], BF16)
        bxl = nc.alloc_sbuf_tensor("bxl", [128, G * tc], BF16)
        bhh = nc.alloc_sbuf_tensor("bhh", [128, G * tc], BF16)
        bhl = nc.alloc_sbuf_tensor("bhl", [128, G * tc], BF16)
        pstage = nc.alloc_sbuf_tensor("pstage", [128, C], F32)
        bident = nc.alloc_sbuf_tensor("bident", [128, 128], F32)
        bbi = nc.alloc_sbuf_tensor("bbi", [128, G], F32)
        bbr = nc.alloc_sbuf_tensor("bbr", [128, G], F32)
        brwh = nc.alloc_sbuf_tensor("brwh", [128, G * E], BF16)
        brwl = nc.alloc_sbuf_tensor("brwl", [128, G * E], BF16)
        blam = nc.alloc_sbuf_tensor("blam", [128, 1], F32)
        beps = nc.alloc_sbuf_tensor("beps", [128, 1], F32)
        bone = nc.alloc_sbuf_tensor("bone", [128, 1], F32)
        bcsel = nc.alloc_sbuf_tensor("bcsel", [128, NCORES], F32)
        bsilu = nc.alloc_sbuf_tensor("bsilu", [128, 512], F32)
        bhf = pstage
        bcarL = nc.alloc_sbuf_tensor("bcarL", [128, G], F32)
        bcarP = nc.alloc_sbuf_tensor("bcarP", [128, G], F32)
        bsP = nc.alloc_sbuf_tensor("bsP", [128, G], F32)
        btmpP = nc.alloc_sbuf_tensor("btmpP", [128, G], F32)
        bHrun = nc.alloc_sbuf_tensor("bHrun", [128, G], F32)
        bag = bx
        bcand_o, bhp_o, btmp8_o = 128, 192, 208
        bsum = nc.alloc_sbuf_tensor("bsum", [128, NT], F32)
        bssq = nc.alloc_sbuf_tensor("bssq", [128, NT], F32)
        bnm = nc.alloc_sbuf_tensor("bnm", [128, NT], F32)
        bm2t = nc.alloc_sbuf_tensor("bm2t", [128, NT], F32)
        bvar = nc.alloc_sbuf_tensor("bvar", [128, NT], F32)
        bstd = nc.alloc_sbuf_tensor("bstd", [128, NT], F32)
        brstd = nc.alloc_sbuf_tensor("brstd", [128, NT], F32)
        bnm1 = nc.alloc_sbuf_tensor("bnm1", [128, NT], F32)
        bnm2 = nc.alloc_sbuf_tensor("bnm2", [128, NT], F32)
        bgs = nc.alloc_sbuf_tensor("bgs", [128, NT], F32)
        bgr = nc.alloc_sbuf_tensor("bgr", [128, NT], F32)
        bge = nc.alloc_sbuf_tensor("bge", [128, NT * E], F32)
        bmsk = nc.alloc_sbuf_tensor("bmsk", [128, NT * E], F32)
        bsel = bmsk
        bgate = nc.alloc_sbuf_tensor("bgate", [128, NT * E], F32)

        pb = [nc.alloc_psum_tensor("pb%d" % i, [128, 512], F32) for i in range(8)]

        S = Sched(nc, es)
        block = es.enter_context(nc.Block())

        bx3 = bx.ap().rearrange("p (u c) -> p u c", c=C)
        bacc3 = bacc.ap().rearrange("p (u c) -> p u c", c=C)
        bacc_uc = bacc3
        bit_uc = bit.ap().rearrange("p (u c) -> p u c", c=C)
        bat3 = bat.ap().rearrange("p (g t) -> p g t", t=tc)
        bgx3 = bgx.ap().rearrange("p (g t) -> p g t", t=tc)
        bxf3 = bxf.ap().rearrange("p (g t) -> p g t", t=tc)
        bxh3 = bxh.ap().rearrange("p (g t) -> p g t", t=tc)
        bxl3 = bxl.ap().rearrange("p (g t) -> p g t", t=tc)
        bxf_uc = bxf.ap().rearrange("p (u c) -> p u c", c=C)
        wS2 = bxf.ap().bitcast(BF16)

        t_bi = S.dma("sync", "d_w", bbi[:, :], bi_d)
        t_br = S.dma("sync", "d_w", bbr[:, :], br_d)

        t_lam = S.dma("sync", "d_w", blam[:, :], lam_d)
        t_cs = S.dma("sync", "d_w", bcsel[:, :], csel_d)

        t_id0 = S.op("gpsimd", lambda q: q.memset(bident[:, :], 1.0))
        t_id = S.op("gpsimd", lambda q: q.affine_select(
            bident[:, :], bident[:, :], pattern=[[1, 128]],
            compare_op=ALU.is_equal, fill=0.0, channel_multiplier=-1),
            deps=[t_id0])
        t_eps = S.op("vector", lambda q: q.memset(beps[:, :], EPS))
        t_one = S.op("vector", lambda q: q.memset(bone[:, :], 1.0))

        t_c1 = S.op("scalar", lambda q: q.activation(blam[:, :], blam[:, :], AF.Exp),
                    deps=[t_lam])
        t_c2 = S.op("scalar", lambda q: q.activation(blam[:, :], blam[:, :], AF.Ln,
                                                     bias=bone[:, 0:1]),
                    deps=[t_c1, t_one])
        t_c = S.op("scalar", lambda q: q.mul(blam[:, :], blam[:, :], -8.0), deps=[t_c2])

        t_rw = S.dma("sync", "d_w",
                     pstage.ap()[:, :G * E].rearrange("p (g e) -> p g e", e=E),
                     rw_v)
        t_rwh = S.op("scalar", lambda q: q.activation(
            brwh[:, :], pstage[:, :G * E], AF.Identity), deps=[t_rw])
        t_rwl = S.op("vector", lambda q: q.tensor_tensor(
            brwl[:, :], pstage[:, :G * E], brwh[:, :], op=ALU.subtract),
            deps=[t_rwh])

        wBh_v = wBh.ap().rearrange("p (w cg d) -> p w cg d", w=2, cg=G)
        wBl_v = wBl.ap().rearrange("p (w cg d) -> p w cg d", w=2, cg=G)

        pc = [0]
        p_last = {}

        def split_common(src):
            i = pc[0]
            pc[0] += 1
            slot = bacc[:, (i % 4) * 1024:(i % 4) * 1024 + 1024]
            t_ld = S.dma("scalar", "d_ps%d" % (i % 4), slot, src,
                         deps=[p_last.get(("sub", i - 4))])
            p_last[("ld", i)] = t_ld
            return i, slot, t_ld

        def wsplit_res(src, dst_h, dst_l):
            i, slot, t_ld = split_common(src)
            t_cast = S.op("gpsimd", lambda q, d=dst_h, sl=slot: q.tensor_copy(
                d, sl), deps=[t_ld, p_last.get(("cast", i - 1))])
            p_last[("cast", i)] = t_cast
            t_sub = S.op("gpsimd", lambda q, d=dst_l, h=dst_h, sl=slot:
                         q.tensor_tensor(d, sl, h, op=ALU.subtract),
                         deps=[t_cast, p_last.get(("sub", i - 1))])
            p_last[("sub", i)] = t_sub
            return t_sub

        def wsplit_one(src, dst_hl):
            i, slot, t_ld = split_common(src)
            sbase = (i % 2) * 2048
            hs = wS[:, sbase:sbase + 1024]
            ls = wS[:, sbase + 1024:sbase + 2048]
            t_cast = S.op("gpsimd", lambda q, d=hs, sl=slot: q.tensor_copy(
                d, sl), deps=[t_ld, p_last.get(("cast", i - 1)),
                              p_last.get(("out", i - 2))])
            p_last[("cast", i)] = t_cast
            t_sub = S.op("gpsimd", lambda q, d=ls, h=hs, sl=slot:
                         q.tensor_tensor(d, sl, h, op=ALU.subtract),
                         deps=[t_cast, p_last.get(("sub", i - 1))])
            p_last[("sub", i)] = t_sub
            out = S.dma("gpsimd", "d_po%d" % (i % 2), dst_hl,
                        wS[:, sbase:sbase + 2048], deps=[t_sub])
            p_last[("out", i)] = out
            return out

        split_jobs = []
        for e_ in range(E):
            for hh in range(2):
                for cg_ in range(G):
                    split_jobs.append((e_, up_v[e_][hh][:, cg_, :],
                                       sc_u_p[e_][hh][:, cg_, :, :]))
            for dg_ in range(G):
                split_jobs.append((e_, dn_v[e_][:, dg_, :],
                                   sc_d_p[e_][:, dg_, :, :]))
        sj_pos = [0]
        t_split_done = [[] for _ in range(E)]

        def pump_splits(n):
            for _ in range(n):
                if sj_pos[0] >= len(split_jobs):
                    return
                e_, src, dhl = split_jobs[sj_pos[0]]
                sj_pos[0] += 1
                t_split_done[e_] = [wsplit_one(src, dhl)]

        t_wsp_all = []
        for wi, wv in ((0, wiT_v), (1, wrT_v)):
            for cg_ in range(G):
                t_wsp_all.append(wsplit_res(
                    wv[:, cg_, :],
                    wBh_v[:, wi, cg_, :],
                    wBl_v[:, wi, cg_, :]))


        def ln_block(src_ap, dump, norm_dst, deps_in, war_deps,
                     extra_norm_deps):
            t_d = None
            for u in range(NT):
                t_d = S.op("scalar", lambda q, u=u: q.activation(
                    dump, src_ap[:, u * C:(u + 1) * C], AF.Square,
                    accum_out=bssq[:, u:u + 1]),
                    deps=(deps_in + war_deps if u == 0 else [t_d]))
                t_d = S.op("scalar", lambda q, u=u: q.activation(
                    dump, src_ap[:, u * C:(u + 1) * C], AF.Identity,
                    accum_out=bsum[:, u:u + 1]), deps=[t_d])
            t_sq = t_sums = t_ssq = t_d
            t_nm = S.op("vector", lambda q: q.tensor_scalar(
                bnm[:, :], bsum[:, :], -1.0 / C, None, op0=ALU.mult), deps=[t_sums])
            t_m2 = S.op("vector", lambda q: q.tensor_tensor(
                bm2t[:, :], bnm[:, :], bnm[:, :], op=ALU.mult), deps=[t_nm])
            t_var = S.op("vector", lambda q: q.scalar_tensor_tensor(
                bvar[:, :], bssq[:, :], 1.0 / C, bm2t[:, :],
                op0=ALU.mult, op1=ALU.subtract), deps=[t_ssq, t_m2])
            t_std = S.op("scalar", lambda q: q.activation(
                bstd[:, :], bvar[:, :], AF.Sqrt, bias=beps[:, 0:1]),
                deps=[t_var, t_eps])
            t_rstd = S.op("vector", lambda q: q.reciprocal(
                brstd[:, :], bstd[:, :]), deps=[t_std])
            t_norm = []
            for u in range(NT):
                t_norm.append(S.op("vector", lambda q, u=u, s=src_ap,
                                   d=norm_dst: q.tensor_scalar(
                                       d[:, u * C:(u + 1) * C],
                                       s[:, u * C:(u + 1) * C],
                                       bnm[:, u:u + 1], brstd[:, u:u + 1],
                                       op0=ALU.add, op1=ALU.mult),
                                   deps=[t_rstd] + list(extra_norm_deps)))
            return t_norm, t_d, t_var

        t_evac = {}
        t_s_psum = [None]

        def tmfm_split_transpose(src, t_src_ready, dst_war, want_f32,
                                 banks=None):
            t_cp = [[None] * 2 for _ in range(NT)]
            if banks is None:
                banks = (pb[0], pb[1])
            for u in range(NT):
                for hb in range(2):
                    bank = banks[(u * 2 + hb) % len(banks)]
                    for k in range(4):
                        g = hb * 4 + k
                        S.op("tensor", lambda q, u=u, g=g, k=k, bank=bank, s=src:
                             q.transpose(bank[:, k * 128:(k + 1) * 128],
                                         s[:, u * C + g * 128:u * C + (g + 1) * 128],
                                         bident[:, :]),
                             deps=([t_src_ready[u], t_id, t_evac.get(id(bank)),
                                    t_s_psum[0]]
                                   if k == 0 else []),
                             inc=(1 if k == 3 else 0))
                    t_tp = S.tok("tensor")
                    sl = (slice(None), slice(hb * 4, hb * 4 + 4),
                          slice(u * 128, (u + 1) * 128))
                    bank3 = bank.ap().rearrange("p (k t) -> p k t", t=128)
                    deps_c = [t_tp] + list(dst_war)
                    t_c32 = None
                    if want_f32:
                        t_c32 = S.op("vector", lambda q, d=bxf3[sl], b=bank3:
                                     q.tensor_copy(d, b), deps=deps_c)
                    t_ch = S.op("vector", lambda q, d=bxh3[sl], b=bank3:
                                q.tensor_copy(d, b), deps=deps_c)
                    if want_f32:
                        t_cl = S.op("vector", lambda q, d=bxl3[sl],
                                    s32=bxf3[sl], h=bxh3[sl]:
                                    q.tensor_tensor(
                                        d, s32, h, op=ALU.subtract),
                                    deps=[t_c32, t_ch])
                    else:
                        t_cl = S.op("vector", lambda q, d=bxl3[sl], b=bank3,
                                    h=bxh3[sl]: q.tensor_tensor(
                                        d, b, h, op=ALU.subtract),
                                    deps=[t_ch])
                    t_evac[id(bank)] = t_cl
                    t_cp[u][hb] = t_cl
            return t_cp

        t_bx_free = None
        t_bit_r_free = None
        t_bsplit_war = [t_rwl]
        t_bgx_r_free = None
        t_sp_at_prev = None
        t_sp_gx_prev = None
        t_carL = [None] * G
        t_eP_prev = None
        t_carP_all = None
        t_scr_free = None

        t_lx_pend = S.dma("sync", "d_x", bx3, x_v[0], deps=[t_bx_free])
        t_norm_pend, t_sqd_pend, t_var_prev = ln_block(
            bx.ap(), pstage.ap()[:, :], bx.ap(),
            deps_in=[t_lx_pend], war_deps=[t_rwl], extra_norm_deps=[])
        TB6 = [pb[0], pb[1], pb[2], pb[5], pb[6], pb[7]]
        t_ixg_last = None
        for q in range(CH):
            t_lx = t_lx_pend
            t_norm = t_norm_pend
            t_fmcp = tmfm_split_transpose(
                bx.ap(), t_norm, t_bsplit_war + [t_ixg_last], True, banks=TB6)
            t_fm_all = [t_fmcp[u][hb] for u in range(NT) for hb in range(2)]

            t_sig = {0: [None] * G, 1: [None] * G}
            t_ixg = [None] * G
            t_red = [None] * G
            t_at = [None] * G
            t_ngg = [None] * G
            for wi in range(2):
                for dg in range(G):
                    bank = pb[3 + (dg % 2)]
                    passes = (((wBh, bxh), (wBh, bxl), (wBl, bxh))
                              if "no3p" not in _DBG else ((wBh, bxh),))
                    npss = len(passes) - 1
                    for cg in range(G):
                        for ps, (wb, xb) in enumerate(passes):
                            deps = []
                            if cg == 0 and ps == 0:
                                deps = t_fm_all + t_wsp_all + \
                                    [t_evac.get(id(bank))]
                            S.op("tensor", lambda q_, wi=wi, dg=dg, cg=cg,
                                 ps=ps, bank=bank, wb=wb, xb=xb:
                                 q_.matmul(
                                     bank[:, :tc],
                                     wb[:, wi * G * C + cg * C + dg * 128:
                                        wi * G * C + cg * C + (dg + 1) * 128],
                                     xb[:, cg * tc:(cg + 1) * tc],
                                     start=(cg == 0 and ps == 0),
                                     stop=(cg == G - 1 and ps == npss)),
                                 deps=deps,
                                 inc=(1 if (cg == G - 1 and ps == npss) else 0))
                    t_mm = S.tok("tensor")
                    if wi == 0:
                        t_s = S.op("scalar", lambda q_, dg=dg, bank=bank:
                                   q_.activation(
                                       bit[:, dg * tc:(dg + 1) * tc],
                                       bank[:, :tc], AF.Sigmoid,
                                       bias=bbi[:, dg:dg + 1]),
                                   deps=[t_mm, t_bi, t_bit_r_free])
                    else:
                        t_s = S.op("scalar", lambda q_, dg=dg, bank=bank:
                                   q_.activation(
                                       bat[:, dg * tc:(dg + 1) * tc],
                                       bank[:, :tc], AF.Sigmoid,
                                       bias=bbr[:, dg:dg + 1]),
                                   deps=[t_mm, t_br, t_sp_at_prev])
                    t_sig[wi][dg] = t_s
                    t_evac[id(bank)] = t_s
                    t_s_psum[0] = t_s
                    pump_splits(1)
                    if wi == 0:
                        t_ixg[dg] = S.op("vector", lambda q_, dg=dg:
                                         q_.tensor_tensor(
                                             bit[:, dg * tc:(dg + 1) * tc],
                                             bit[:, dg * tc:(dg + 1) * tc],
                                             bxf[:, dg * tc:(dg + 1) * tc],
                                             op=ALU.mult), deps=[t_s])
                    else:
                        t_red[dg] = S.op("vector", lambda q_, dg=dg:
                                         q_.reduce_sum(
                                             bsP[:, dg:dg + 1],
                                             bat[:, dg * tc:(dg + 1) * tc],
                                             axis=AX.X),
                                         deps=[t_s, t_eP_prev])
                        t_at[dg] = S.op("scalar", lambda q_, dg=dg:
                                        q_.activation(
                                            bat[:, dg * tc:(dg + 1) * tc],
                                            bat[:, dg * tc:(dg + 1) * tc],
                                            AF.Exp, scale=blam[:, 0:1]),
                                        deps=[t_s, t_c, t_red[dg]])
                        t_ngg[dg] = S.op("vector", lambda q_, dg=dg:
                                         q_.scalar_tensor_tensor(
                                             bgx[:, dg * tc:(dg + 1) * tc],
                                             bat[:, dg * tc:(dg + 1) * tc],
                                             -1.0,
                                             bat[:, dg * tc:(dg + 1) * tc],
                                             op0=ALU.mult, op1=ALU.mult),
                                         deps=[t_at[dg], t_sp_gx_prev,
                                               t_bgx_r_free])
            t_ixg_last = t_ixg[G - 1]
            t_pa_tok = S.tok("tensor")
            t_eP = S.op("scalar", lambda q_: q_.activation(
                btmpP[:, :], bsP[:, :], AF.Exp, scale=blam[:, 0:1]),
                deps=t_red + [t_carP_all])
            t_eP_prev = t_eP
            if q == 0:
                t_carP_all = S.op("vector", lambda q_: q_.tensor_copy(
                    bcarP[:, :], btmpP[:, :]), deps=[t_eP])
            else:
                t_carP_all = S.op("vector", lambda q_: q_.tensor_tensor(
                    bcarP[:, :], bcarP[:, :], btmpP[:, :], op=ALU.mult),
                    deps=[t_eP, t_carP_all])
            t_cfg = S.op("scalar", lambda q_: q_.activation(
                bgx[:, :], bgx[:, :], AF.Sqrt, bias=bone[:, 0:1]),
                deps=t_ngg + [t_one])
            pump_splits(8)
            if q + 1 < CH:
                t_lx_pend = S.dma("sync", "d_x", bx3, x_v[q + 1],
                                  deps=[t_fm_all[-1]])
                t_norm_pend, t_sqd_pend, t_var_prev = ln_block(
                    bx.ap(), pstage.ap()[:, :], bx.ap(),
                    deps_in=[t_lx_pend, t_var_prev], war_deps=[],
                    extra_norm_deps=[])
            t_gxg = [None] * G
            for g in range(G):
                t_gxg[g] = S.op("vector", lambda q_, g=g: q_.tensor_tensor(
                    bgx[:, g * tc:(g + 1) * tc], bgx[:, g * tc:(g + 1) * tc],
                    bit[:, g * tc:(g + 1) * tc], op=ALU.mult),
                    deps=[t_cfg, t_ixg[g]])
                initL = 0.0 if q == 0 else bcarL[:, g:g + 1]
                t_ls = S.op("vector", lambda q_, g=g, initL=initL:
                            q_.tensor_tensor_scan(
                                bsilu[:, :], bat[:, g * tc:(g + 1) * tc],
                                bgx[:, g * tc:(g + 1) * tc], initL,
                                op0=ALU.mult, op1=ALU.add),
                            deps=[t_gxg[g], t_carL[g], t_scr_free])
                t_carL[g] = S.op("vector", lambda q_, g=g: q_.tensor_copy(
                    bcarL[:, g:g + 1], bsilu[:, tc - 1:tc]), deps=[t_ls])
                t_scr_free = t_carL[g]

            t_sp_at_prev = S.dma("sync", "d_spat", spat_v[q], bat3, deps=t_at)
            t_sp_gx_prev = S.dma("sync", "d_spgx", spgx_v[q], bgx3,
                                 deps=list(t_gxg))
            t_bit_r_free = t_gxg[G - 1]
            t_bsplit_war = [t_pa_tok]
            t_bgx_r_free = t_carL[G - 1]
            t_bx_free = t_fm_all[-1]

        t_e1 = S.dma("sync", "d_ag", agin.ap()[0], bcarP[:, :],
                     deps=[t_carP_all])
        t_e2 = S.dma("sync", "d_ag", agin.ap()[1], bcarL[:, :],
                     deps=[t_carL[g] for g in range(G)])
        t_cc = S.op("gpsimd", lambda q_: q_.collective_compute(
            "AllGather", ALU.bypass, replica_groups=[list(range(NCORES))],
            ins=[agin.ap().opt()], outs=[agout.ap().opt()]),
            deps=[t_e1, t_e2], sem="s_cc")
        t_bag = S.dma("sync", "d_bag",
                      bag.ap()[:, :NCORES * 2 * G].rearrange(
                          "p (r t g) -> p r t g", t=2, g=G),
                      agout_v, deps=[t_cc, t_bx_free])

        def cnd(i):
            return bag[:, bcand_o + i * G:bcand_o + (i + 1) * G]

        t_cand = [None] * NCORES
        t_cand[0] = S.op("vector", lambda q_: q_.memset(cnd(0), 0.0),
                         deps=[t_bag])
        t_cand[4] = S.op("vector", lambda q_: q_.memset(cnd(4), 0.0),
                         deps=[t_bag])
        t_btmp_r = None
        for b in range(2):
            for k in range(1, 4):
                i = 4 * b + k
                r = 4 * b + k - 1
                tm = S.op("vector", lambda q_, i=i, r=r: q_.tensor_tensor(
                    bag[:, btmp8_o:btmp8_o + G], bag[:, r * 16:r * 16 + G],
                    cnd(i - 1), op=ALU.mult),
                    deps=[t_bag, t_cand[i - 1], t_btmp_r])
                t_cand[i] = S.op("vector", lambda q_, i=i, r=r: q_.tensor_tensor(
                    cnd(i), bag[:, r * 16 + G:r * 16 + 2 * G],
                    bag[:, btmp8_o:btmp8_o + G], op=ALU.add), deps=[tm])
                t_btmp_r = t_cand[i]
        t_h = S.op("vector", lambda q_: q_.tensor_scalar(
            bag[:, bhp_o:bhp_o + G], cnd(0), bcsel[:, 0:1], None, op0=ALU.mult),
            deps=[t_cand[0], t_cs])
        for i in range(1, NCORES):
            src = ((i + 1) % 2) * G
            dst = (i % 2) * G
            t_h = S.op("vector", lambda q_, i=i, src=src, dst=dst:
                       q_.scalar_tensor_tensor(
                           bag[:, bhp_o + dst:bhp_o + dst + G], cnd(i),
                           bcsel[:, i:i + 1], bag[:, bhp_o + src:bhp_o + src + G],
                           op0=ALU.mult, op1=ALU.add),
                       deps=[t_cand[i], t_h])
        t_hrun = S.op("vector", lambda q_: q_.tensor_copy(
            bHrun[:, :], bag[:, bhp_o + G:bhp_o + 2 * G]), deps=[t_h])

        pump_splits(len(split_jobs))

        t_pa_pe = S.tok("tensor")
        t_lat_free = t_sp_at_prev
        t_lgx_free = t_sp_gx_prev
        t_bx_free_b = t_bx_free
        t_bit_free = t_bit_r_free
        t_ws2_war = [t_bit_r_free]
        t_hup = [t_hrun] * G
        t_out_prev = p_last.get(("sub", pc[0] - 1))
        t_up_used = {0: t_pa_pe, 1: t_pa_pe}
        t_dn_war = {0: [p_last.get(("out", pc[0] - 1)),
                        p_last.get(("out", pc[0] - 2))],
                    1: [t_bit_r_free]}
        t_hid_prev = t_scr_free
        t_bhid_war = None
        t_bhf_war = t_sqd_pend
        sp_at_all = ("d_spat", 16 * CH)
        sp_gx_all = ("d_spgx", 16 * CH)

        for q in range(CH):
            t_lat = S.dma("scalar", "d_lat", bat3, spat_v[q],
                          deps=[sp_at_all, t_lat_free])
            t_lgx = S.dma("scalar", "d_lgx", bgx3, spgx_v[q],
                          deps=[sp_gx_all, t_lgx_free])
            t_lx = S.dma("sync", "d_x", bx3, x_v[q],
                         deps=[t_bx_free_b, t_hrun if q == 0 else None])

            t_scan = [None] * G
            for g in range(G):
                t_scan[g] = S.op("vector", lambda q_, g=g: q_.tensor_tensor_scan(
                    bit[:, g * tc:(g + 1) * tc], bat[:, g * tc:(g + 1) * tc],
                    bgx[:, g * tc:(g + 1) * tc], bHrun[:, g:g + 1],
                    op0=ALU.mult, op1=ALU.add),
                    deps=[t_lat, t_lgx, t_hup[g], t_bit_free])
                t_hup[g] = S.op("vector", lambda q_, g=g: q_.tensor_copy(
                    bHrun[:, g:g + 1], bit[:, g * tc + tc - 1:g * tc + tc]),
                    deps=[t_scan[g]])
            t_lat_free = t_scan[G - 1]
            t_lgx_free = t_scan[G - 1]

            t_res = []
            for u in range(NT):
                for hb in range(2):
                    bank = pb[hb]
                    for k in range(4):
                        g = hb * 4 + k
                        S.op("tensor", lambda q_, u=u, g=g, k=k, bank=bank:
                             q_.transpose(bank[:, k * 128:(k + 1) * 128],
                                          bit[:, g * tc + u * 128:g * tc + (u + 1) * 128],
                                          bident[:, :]),
                             deps=([t_scan[hb * 4 + kk] for kk in range(4)]
                                   + [t_id, t_evac.get(id(bank)), t_s_psum[0]]
                                   if k == 0 else []),
                             inc=(1 if k == 3 else 0))
                    t_tp = S.tok("tensor")
                    t_r = S.op("vector", lambda q_, u=u, hb=hb, bank=bank:
                               q_.tensor_tensor(
                                   bacc[:, u * C + hb * 512:u * C + (hb + 1) * 512],
                                   bank[:, :512],
                                   bx[:, u * C + hb * 512:u * C + (hb + 1) * 512],
                                   op=ALU.add),
                               deps=[t_tp, t_lx, t_out_prev])
                    t_evac[id(bank)] = t_r
                    t_res.append(t_r)
            t_bx_free_b = t_res[-1]

            t_norm, t_sqd2, t_var_prev = ln_block(
                bacc.ap(), bgx.ap()[:, :C], bit.ap(),
                deps_in=t_res + [t_var_prev], war_deps=[t_scan[G - 1]],
                extra_norm_deps=[t_hup[G - 1]])
            t_lgx_free = t_sqd2

            t_x2f = tmfm_split_transpose(bit.ap(), t_norm, t_bsplit_war, False)
            t_x2f_all = [t_x2f[u][hb] for u in range(NT) for hb in range(2)]

            t_gate = [None] * NT
            prbank = pb[2]
            for u in range(NT if "norouter" not in _DBG else 0):
                for cg in range(G):
                    for ps, (xb, rb) in enumerate(
                            ((bxh, brwh), (bxl, brwh), (bxh, brwl))):
                        deps = []
                        if cg == 0 and ps == 0:
                            deps = [t_x2f[u][0], t_x2f[u][1], t_rwl,
                                    t_evac.get(id(prbank))]
                        S.op("tensor", lambda q_, u=u, cg=cg, ps=ps, xb=xb, rb=rb:
                             q_.matmul(
                                 prbank[:, :E],
                                 xb[:, cg * tc + u * 128:cg * tc + (u + 1) * 128],
                                 rb[:, cg * E:(cg + 1) * E],
                                 start=(cg == 0 and ps == 0),
                                 stop=(cg == G - 1 and ps == 2)),
                             deps=deps,
                             inc=(1 if (cg == G - 1 and ps == 2) else 0))
                t_rmm = S.tok("tensor")
                t_m1 = S.op("vector", lambda q_, u=u: q_.reduce_max(
                    bnm1[:, u:u + 1], prbank[:, :E], axis=AX.X, negate=True),
                    deps=[t_rmm])
                t_ex = S.op("scalar", lambda q_, u=u: q_.activation(
                    bge[:, u * E:(u + 1) * E], prbank[:, :E], AF.Exp,
                    bias=bnm1[:, u:u + 1]), deps=[t_m1, t_x2f_all[-1]])
                t_s_psum[0] = t_ex
                t_mk = S.op("vector", lambda q_, u=u: q_.tensor_scalar(
                    bmsk[:, u * E:(u + 1) * E], prbank[:, :E],
                    bnm1[:, u:u + 1], 0.0, op0=ALU.add, op1=ALU.is_ge),
                    deps=[t_m1])
                t_mk2 = S.op("vector", lambda q_, u=u: q_.scalar_tensor_tensor(
                    bmsk[:, u * E:(u + 1) * E], bmsk[:, u * E:(u + 1) * E],
                    -BIG, prbank[:, :E], op0=ALU.mult, op1=ALU.add),
                    deps=[t_mk])
                t_m2r = S.op("vector", lambda q_, u=u: q_.reduce_max(
                    bnm2[:, u:u + 1], bmsk[:, u * E:(u + 1) * E], axis=AX.X,
                    negate=True), deps=[t_mk2])
                t_se = S.op("vector", lambda q_, u=u: q_.tensor_scalar(
                    bsel[:, u * E:(u + 1) * E], prbank[:, :E],
                    bnm2[:, u:u + 1], 0.0, op0=ALU.add, op1=ALU.is_ge),
                    deps=[t_m2r])
                t_g0 = S.op("vector", lambda q_, u=u: q_.tensor_tensor(
                    bge[:, u * E:(u + 1) * E], bge[:, u * E:(u + 1) * E],
                    bsel[:, u * E:(u + 1) * E], op=ALU.mult), deps=[t_se, t_ex])
                t_evac[id(prbank)] = t_g0
                t_gs = S.op("vector", lambda q_, u=u: q_.reduce_sum(
                    bgs[:, u:u + 1], bge[:, u * E:(u + 1) * E], axis=AX.X),
                    deps=[t_g0])
                t_gr = S.op("vector", lambda q_, u=u: q_.reciprocal(
                    bgr[:, u:u + 1], bgs[:, u:u + 1]), deps=[t_gs])
                t_gate[u] = S.op("vector", lambda q_, u=u: q_.tensor_scalar(
                    bgate[:, u * E:(u + 1) * E], bge[:, u * E:(u + 1) * E],
                    bgr[:, u:u + 1], None, op0=ALU.mult), deps=[t_gr])

            t_acc_last = t_res
            for e in range(E if "noffn" not in _DBG else 0):
                sd = t_split_done[e]
                sc_u_e = sc_u[e].rearrange("p (pp hl d1) -> p hl pp d1", hl=2, d1=1024)
                t_upA = S.dma("sync", "d_uph",
                              wBh.ap().rearrange("p (pp d1) -> p pp d1", d1=C),
                              sc_u_e[:, 0], deps=[t_up_used[0]] + sd)
                t_upB = S.dma("sync", "d_upl",
                              wBl.ap().rearrange("p (pp d1) -> p pp d1", d1=C),
                              sc_u_e[:, 1], deps=[t_up_used[1]] + sd)
                t_hid = [None] * G
                for dk in range(G):
                    for which, bank in ((0, pb[3]), (1, pb[4])):
                        for cg in range(G):
                            for ps, (wb, xb) in enumerate(
                                    ((wBh, bxh), (wBh, bxl), (wBl, bxh))):
                                deps = []
                                if cg == 0 and ps == 0:
                                    deps = t_x2f_all + [t_upA, t_upB,
                                                       t_evac.get(id(bank))]
                                S.op("tensor", lambda q_, dk=dk, cg=cg, ps=ps,
                                     which=which, bank=bank, wb=wb, xb=xb:
                                     q_.matmul(
                                         bank[:, :512],
                                         wb[:, which * G * C + cg * C + dk * 128:
                                            which * G * C + cg * C + (dk + 1) * 128],
                                         xb[:, cg * tc:(cg + 1) * tc],
                                         start=(cg == 0 and ps == 0),
                                         stop=(cg == G - 1 and ps == 2)),
                                     deps=deps,
                                     inc=(1 if (cg == G - 1 and ps == 2) else 0))
                        if which == 0:
                            t_h1 = S.tok("tensor")
                        else:
                            t_h2 = S.tok("tensor")
                    t_si = S.op("scalar", lambda q_: q_.activation(
                        bsilu[:, :], pb[3][:, :512], AF.Silu),
                        deps=[t_h1, t_hid_prev])
                    t_evac[id(pb[3])] = t_si
                    t_s_psum[0] = t_si
                    t_tt = S.op("vector", lambda q_: q_.tensor_tensor(
                        bhf[:, :512], bsilu[:, :], pb[4][:, :512], op=ALU.mult),
                        deps=[t_h2, t_si, t_bhf_war, t_scan[G - 1]])
                    t_evac[id(pb[4])] = t_tt
                    t_hh2 = S.op("scalar", lambda q_, dk=dk: q_.activation(
                        bhh[:, dk * 512:(dk + 1) * 512], bhf[:, :512],
                        AF.Identity), deps=[t_tt, t_bhid_war])
                    t_hl2 = S.op("vector", lambda q_, dk=dk: q_.tensor_tensor(
                        bhl[:, dk * 512:(dk + 1) * 512], bhf[:, :512],
                        bhh[:, dk * 512:(dk + 1) * 512], op=ALU.subtract),
                        deps=[t_hh2])
                    t_bhf_war = t_hl2
                    t_hid[dk] = t_hl2
                    t_hid_prev = t_tt
                    t_bhf_war = t_hl2

                t_up_done = S.tok("tensor")

                t_dmm = {}
                t_dld = {}
                for r in range(4):
                    rb = wS.ap() if r % 2 == 0 else wS2[:, :4096]
                    t_dh = S.dma("sync", "d_dnh",
                                 rb[:, :2048].rearrange(
                                     "p (k d) -> p k d", k=2),
                                 sc_d_p[e][:, 2 * r:2 * r + 2, 0, :],
                                 deps=t_dn_war[r % 2])
                    t_dl = S.dma("sync", "d_dnl",
                                 rb[:, 2048:4096].rearrange(
                                     "p (k d) -> p k d", k=2),
                                 sc_d_p[e][:, 2 * r:2 * r + 2, 1, :],
                                 deps=t_dn_war[r % 2])
                    t_dld[r] = [t_dh, t_dl]
                    for u in range(NT):
                        for hc in range(2):
                            bank = pb[u * 2 + hc]
                            for dgl in range(2):
                                dg = 2 * r + dgl
                                for ps, (hb_, dof) in enumerate(
                                        ((bhh, 0), (bhh, 2048), (bhl, 0))):
                                    deps = []
                                    if r == 0 and dgl == 0 and ps == 0:
                                        deps = [t_dh, t_dl,
                                                t_evac.get(id(bank))] + t_hid
                                    elif dgl == 0 and ps == 0 and u == 0 and hc == 0:
                                        deps = [t_dh, t_dl]
                                    S.op("tensor", lambda q_, u=u, hc=hc, dg=dg,
                                         dgl=dgl, r=r, ps=ps, bank=bank,
                                         hb_=hb_, dof=dof, rb=rb:
                                         q_.matmul(
                                             bank[:, :512],
                                             hb_[:, dg * 512 + u * 128:
                                                 dg * 512 + (u + 1) * 128],
                                             rb[:, dof + dgl * C + hc * 512:
                                                dof + dgl * C + (hc + 1) * 512],
                                             start=(r == 0 and dgl == 0 and ps == 0),
                                             stop=(r == 3 and dgl == 1 and ps == 2)),
                                         deps=deps,
                                         inc=(1 if (dgl == 1 and ps == 2
                                                    and ((u == NT - 1 and hc == 1)
                                                         or r == 3))
                                              else 0))
                                if r == 3 and dgl == 1:
                                    t_dmm[(u, hc)] = S.tok("tensor")
                    t_dn_war[r % 2] = [S.tok("tensor")]
                t_acc_new = [None] * 8
                for u, hc in ((0, 0), (0, 1), (1, 0), (1, 1), (2, 0), (2, 1),
                              (3, 0), (3, 1)):
                    bank = pb[u * 2 + hc]
                    t_a = S.op("vector", lambda q_, u=u, hc=hc, e=e,
                               bank=bank: q_.scalar_tensor_tensor(
                                   bacc[:, u * C + hc * 512:u * C + (hc + 1) * 512],
                                   bank[:, :512],
                                   bgate[:, u * E + e:u * E + e + 1],
                                   bacc[:, u * C + hc * 512:u * C + (hc + 1) * 512],
                                   op0=ALU.mult, op1=ALU.add),
                               deps=[t_dmm[(u, hc)], t_gate[u],
                                     t_acc_last[u * 2 + hc]])
                    t_evac[id(bank)] = t_a
                    t_acc_new[u * 2 + hc] = t_a
                t_acc_last = t_acc_new
                t_pe_now = S.tok("tensor")
                t_up_used = {0: t_up_done, 1: t_up_done}
                t_bhid_war = t_pe_now
                t_bsplit_war = [t_pe_now]

            t_out_prev = S.dma("sync", "d_out", out_v[q], bacc3,
                               deps=t_acc_last)
            t_lat_free = t_bhf_war
            t_bit_free = t_x2f_all[-1]

        S.fence("sync", [("d_out", 16 * CH), t_out_prev])
        S.build(block)
    return nc


def _stage(inputs, tok):
    x = np.ascontiguousarray(np.asarray(inputs["L0_attn_x"], np.float32)
                             .reshape(-1, C))
    wiT = np.ascontiguousarray(np.asarray(inputs["L0_attn_Wi"], np.float32).T)
    wrT = np.ascontiguousarray(np.asarray(inputs["L0_attn_Wr"], np.float32).T)
    bi = np.ascontiguousarray(np.asarray(inputs["L0_attn_bi"], np.float32)
                              .reshape(G, 128).T)
    br = np.ascontiguousarray(np.asarray(inputs["L0_attn_br"], np.float32)
                              .reshape(G, 128).T)
    lam = np.ascontiguousarray(
        np.tile(np.asarray(inputs["L0_attn_cap_lambda"], np.float32)
                .reshape(1, 1), (128, 1)))
    rw = np.ascontiguousarray(np.asarray(inputs["L0_ffn_router_w"], np.float32).T)
    up = np.ascontiguousarray(np.asarray(inputs["L0_ffn_up_proj"], np.float32))
    dn = np.ascontiguousarray(np.asarray(inputs["L0_ffn_down_proj"], np.float32))
    in_maps = []
    for c in range(NCORES):
        csel = np.ascontiguousarray(
            np.tile(np.eye(NCORES, dtype=np.float32)[c], (128, 1)))
        in_maps.append({
            "x": np.ascontiguousarray(x[c * tok:(c + 1) * tok]),
            "wiT": wiT, "wrT": wrT, "bi": bi, "br": br, "lam": lam,
            "rw": rw, "up": up, "dn": dn, "csel": csel,
        })
    return in_maps


def kernel(**inputs):
    B, T, _ = inputs["L0_attn_x"].shape
    tok = B * T // NCORES
    in_maps = _stage(inputs, tok)
    nc = build_program(tok=tok, tc=512)
    res = run_bass_kernel_spmd(nc, in_maps, core_ids=list(range(NCORES)))
    out = np.concatenate([np.asarray(r["out"]) for r in res.results], axis=0)
    return out.reshape(B, T, C).astype(np.float32)


# version 15 (final)
# speedup vs baseline: 1.0158x; 1.0049x over previous
import contextlib
import os
import numpy as np

_DBG = os.environ.get("KDBG", "")

import concourse.bass as bass
import concourse.mybir as mybir
from concourse.bass_utils import run_bass_kernel_spmd

F32 = mybir.dt.float32
BF16 = mybir.dt.bfloat16
AF = mybir.ActivationFunctionType
ALU = mybir.AluOpType
AX = mybir.AxisListType

NCORES = 8
C = 1024
G = 8
E = 8
DFF = 2048
EPS = 1e-5
BIG = 1e9


class Sched:

    ENGS = ("sync", "tensor", "vector", "scalar", "gpsimd")

    def __init__(self, nc, es):
        self.nc = nc
        self.es = es
        self.items = {e: [] for e in self.ENGS}
        self.cnt = {}
        self.sems = {}
        self.rot = {}

    def _sem(self, name):
        if name not in self.sems:
            self.sems[name] = self.es.enter_context(self.nc.semaphore(name))
            self.cnt[name] = 0
        return self.sems[name]

    def op(self, eng, fn, deps=(), inc=1, sem=None):
        if sem is None:
            sem = "e_" + eng
        self._sem(sem)
        waits = {}
        for d in deps:
            if d is None:
                continue
            s, v = d
            self._sem(s)
            if waits.get(s, 0) < v:
                waits[s] = v
        if inc:
            self.cnt[sem] += inc
        self.items[eng].append((fn, waits, sem if inc else None, inc))
        return (sem, self.cnt[sem]) if inc else None

    def tok(self, eng):
        sem = "e_" + eng
        self._sem(sem)
        return (sem, self.cnt[sem])

    ROT = ()

    def dma(self, eng, sem, out, in_, deps=()):
        def fn(q, out=out, in_=in_):
            return q.dma_start(out=out, in_=in_)
        if sem in self.ROT:
            n = self.rot.get(sem, 0)
            self.rot[sem] = n + 1
            sem = "%s_%d" % (sem, n // 14)
        self._sem(sem)
        deps = list(deps) + [(sem, self.cnt[sem])]
        return self.op(eng, fn, deps=deps, inc=16, sem=sem)

    def fence(self, eng, deps):
        return self.op(eng, lambda q: None, deps=deps, inc=0)

    def build(self, block):
        for eng in self.ENGS:
            items = self.items[eng]
            if not items:
                continue

            def body(q, items=items):
                vclock = {}
                for fn, waits, sem, inc in items:
                    for s in sorted(waits):
                        v = waits[s]
                        if vclock.get(s, 0) < v:
                            q.wait_ge(self.sems[s], v)
                            vclock[s] = v
                    inst = fn(q)
                    if inst is not None and sem is not None:
                        inst.then_inc(self.sems[sem], inc)

            getattr(block, eng)(body)


def build_program(tok=4096, tc=512):
    CH = tok // tc
    NT = tc // 128
    assert NT == 4 and tok % tc == 0

    nc = bass.Bass("TRN2", target_bir_lowering=False, debug=False)

    x_d = nc.dram_tensor("x", [tok, C], F32, kind="ExternalInput").ap()
    wiT_d = nc.dram_tensor("wiT", [C, C], F32, kind="ExternalInput").ap()
    wrT_d = nc.dram_tensor("wrT", [C, C], F32, kind="ExternalInput").ap()
    bi_d = nc.dram_tensor("bi", [128, G], F32, kind="ExternalInput").ap()
    br_d = nc.dram_tensor("br", [128, G], F32, kind="ExternalInput").ap()
    lam_d = nc.dram_tensor("lam", [128, 1], F32, kind="ExternalInput").ap()
    rw_d = nc.dram_tensor("rw", [C, E], F32, kind="ExternalInput").ap()
    up_d = nc.dram_tensor("up", [E, C, DFF], F32, kind="ExternalInput").ap()
    dn_d = nc.dram_tensor("dn", [E, C, C], F32, kind="ExternalInput").ap()
    csel_d = nc.dram_tensor("csel", [128, NCORES], F32, kind="ExternalInput").ap()
    out_d = nc.dram_tensor("out", [tok, C], F32, kind="ExternalOutput").ap()

    sp_at = nc.dram_tensor("sp_at", [CH, G, 128, tc], F32, kind="Internal").ap()
    sp_gx = nc.dram_tensor("sp_gx", [CH, G, 128, tc], F32, kind="Internal").ap()
    agin = nc.dram_tensor("agin", [2, 128, G], F32, kind="Internal")
    agout = nc.dram_tensor("agout", [NCORES, 2, 128, G], F32, kind="Internal")
    sc_u = nc.dram_tensor("sc_u", [E, 128, 2 * G * 2 * C], BF16, kind="Internal").ap()
    sc_d = nc.dram_tensor("sc_d", [E, 128, G * 2 * C], BF16, kind="Internal").ap()

    x_v = x_d.rearrange("(q u p) c -> q p u c", p=128, u=NT)
    out_v = out_d.rearrange("(q u p) c -> q p u c", p=128, u=NT)
    spat_v = sp_at.rearrange("q g p t -> q p g t")
    spgx_v = sp_gx.rearrange("q g p t -> q p g t")
    wiT_v = wiT_d.rearrange("(cg p) d -> p cg d", p=128)
    wrT_v = wrT_d.rearrange("(cg p) d -> p cg d", p=128)
    rw_v = rw_d.rearrange("(cg p) e -> p cg e", p=128)
    up_v = up_d.rearrange("e (cg p) (hh d1) -> e hh p cg d1", p=128, hh=2)
    dn_v = dn_d.rearrange("e (dg p) h -> e p dg h", p=128)
    agout_v = agout.ap().rearrange("r t p g -> p r t g")
    sc_u_p = sc_u.rearrange("e p (hh cg hl d1) -> e hh p cg hl d1", hh=2, cg=G, hl=2)
    sc_d_p = sc_d.rearrange("e p (dg hl h) -> e p dg hl h", dg=G, hl=2)

    es = contextlib.ExitStack()
    with es:
        bx = nc.alloc_sbuf_tensor("bx", [128, NT * C], F32)
        bxf = nc.alloc_sbuf_tensor("bxf", [128, G * tc], F32)
        bit = nc.alloc_sbuf_tensor("bit", [128, G * tc], F32)
        bat = nc.alloc_sbuf_tensor("bat", [128, G * tc], F32)
        bgx = nc.alloc_sbuf_tensor("bgx", [128, G * tc], F32)
        bacc = nc.alloc_sbuf_tensor("bacc", [128, NT * C], F32)
        wBh = nc.alloc_sbuf_tensor("wBh", [128, 2 * G * C], BF16)
        wBl = nc.alloc_sbuf_tensor("wBl", [128, 2 * G * C], BF16)
        wS = nc.alloc_sbuf_tensor("wS", [128, 4 * C], BF16)
        bxh = nc.alloc_sbuf_tensor("bxh", [128, G * tc], BF16)
        bxl = nc.alloc_sbuf_tensor("bxl", [128, G * tc], BF16)
        bhh = nc.alloc_sbuf_tensor("bhh", [128, G * tc], BF16)
        bhl = nc.alloc_sbuf_tensor("bhl", [128, G * tc], BF16)
        pstage = nc.alloc_sbuf_tensor("pstage", [128, C], F32)
        bident = nc.alloc_sbuf_tensor("bident", [128, 128], F32)
        bbi = nc.alloc_sbuf_tensor("bbi", [128, G], F32)
        bbr = nc.alloc_sbuf_tensor("bbr", [128, G], F32)
        brwh = nc.alloc_sbuf_tensor("brwh", [128, G * E], BF16)
        brwl = nc.alloc_sbuf_tensor("brwl", [128, G * E], BF16)
        blam = nc.alloc_sbuf_tensor("blam", [128, 1], F32)
        beps = nc.alloc_sbuf_tensor("beps", [128, 1], F32)
        bone = nc.alloc_sbuf_tensor("bone", [128, 1], F32)
        bcsel = nc.alloc_sbuf_tensor("bcsel", [128, NCORES], F32)
        bsilu = nc.alloc_sbuf_tensor("bsilu", [128, 512], F32)
        bhf = pstage
        bcarL = nc.alloc_sbuf_tensor("bcarL", [128, G], F32)
        bcarP = nc.alloc_sbuf_tensor("bcarP", [128, G], F32)
        bsP = nc.alloc_sbuf_tensor("bsP", [128, G], F32)
        btmpP = nc.alloc_sbuf_tensor("btmpP", [128, G], F32)
        bHrun = nc.alloc_sbuf_tensor("bHrun", [128, G], F32)
        bag = bx
        bcand_o, bhp_o, btmp8_o = 128, 192, 208
        bsum = nc.alloc_sbuf_tensor("bsum", [128, NT], F32)
        bssq = nc.alloc_sbuf_tensor("bssq", [128, NT], F32)
        bnm = nc.alloc_sbuf_tensor("bnm", [128, NT], F32)
        bm2t = nc.alloc_sbuf_tensor("bm2t", [128, NT], F32)
        bvar = nc.alloc_sbuf_tensor("bvar", [128, NT], F32)
        bstd = nc.alloc_sbuf_tensor("bstd", [128, NT], F32)
        brstd = nc.alloc_sbuf_tensor("brstd", [128, NT], F32)
        bnm1 = nc.alloc_sbuf_tensor("bnm1", [128, NT], F32)
        bnm2 = nc.alloc_sbuf_tensor("bnm2", [128, NT], F32)
        bgs = nc.alloc_sbuf_tensor("bgs", [128, NT], F32)
        bgr = nc.alloc_sbuf_tensor("bgr", [128, NT], F32)
        bge = nc.alloc_sbuf_tensor("bge", [128, NT * E], F32)
        bmsk = nc.alloc_sbuf_tensor("bmsk", [128, NT * E], F32)
        bsel = bmsk
        bgate = nc.alloc_sbuf_tensor("bgate", [128, NT * E], F32)

        pb = [nc.alloc_psum_tensor("pb%d" % i, [128, 512], F32) for i in range(8)]

        S = Sched(nc, es)
        block = es.enter_context(nc.Block())

        bx3 = bx.ap().rearrange("p (u c) -> p u c", c=C)
        bacc3 = bacc.ap().rearrange("p (u c) -> p u c", c=C)
        bacc_uc = bacc3
        bit_uc = bit.ap().rearrange("p (u c) -> p u c", c=C)
        bat3 = bat.ap().rearrange("p (g t) -> p g t", t=tc)
        bgx3 = bgx.ap().rearrange("p (g t) -> p g t", t=tc)
        bxf3 = bxf.ap().rearrange("p (g t) -> p g t", t=tc)
        bxh3 = bxh.ap().rearrange("p (g t) -> p g t", t=tc)
        bxl3 = bxl.ap().rearrange("p (g t) -> p g t", t=tc)
        bxf_uc = bxf.ap().rearrange("p (u c) -> p u c", c=C)
        wS2 = bxf.ap().bitcast(BF16)

        t_bi = S.dma("sync", "d_w", bbi[:, :], bi_d)
        t_br = S.dma("sync", "d_w", bbr[:, :], br_d)

        t_lam = S.dma("sync", "d_w", blam[:, :], lam_d)
        t_cs = S.dma("sync", "d_w", bcsel[:, :], csel_d)

        t_id0 = S.op("gpsimd", lambda q: q.memset(bident[:, :], 1.0))
        t_id = S.op("gpsimd", lambda q: q.affine_select(
            bident[:, :], bident[:, :], pattern=[[1, 128]],
            compare_op=ALU.is_equal, fill=0.0, channel_multiplier=-1),
            deps=[t_id0])
        t_eps = S.op("vector", lambda q: q.memset(beps[:, :], EPS))
        t_one = S.op("vector", lambda q: q.memset(bone[:, :], 1.0))

        t_c1 = S.op("scalar", lambda q: q.activation(blam[:, :], blam[:, :], AF.Exp),
                    deps=[t_lam])
        t_c2 = S.op("scalar", lambda q: q.activation(blam[:, :], blam[:, :], AF.Ln,
                                                     bias=bone[:, 0:1]),
                    deps=[t_c1, t_one])
        t_c = S.op("scalar", lambda q: q.mul(blam[:, :], blam[:, :], -8.0), deps=[t_c2])

        t_rw = S.dma("sync", "d_w",
                     pstage.ap()[:, :G * E].rearrange("p (g e) -> p g e", e=E),
                     rw_v)
        t_rwh = S.op("scalar", lambda q: q.activation(
            brwh[:, :], pstage[:, :G * E], AF.Identity), deps=[t_rw])
        t_rwl = S.op("vector", lambda q: q.tensor_tensor(
            brwl[:, :], pstage[:, :G * E], brwh[:, :], op=ALU.subtract),
            deps=[t_rwh])

        wBh_v = wBh.ap().rearrange("p (w cg d) -> p w cg d", w=2, cg=G)
        wBl_v = wBl.ap().rearrange("p (w cg d) -> p w cg d", w=2, cg=G)

        pc = [0]
        p_last = {}

        def split_common(src):
            i = pc[0]
            pc[0] += 1
            slot = bacc[:, (i % 4) * 1024:(i % 4) * 1024 + 1024]
            t_ld = S.dma("scalar", "d_ps%d" % (i % 4), slot, src,
                         deps=[p_last.get(("sub", i - 4))])
            p_last[("ld", i)] = t_ld
            return i, slot, t_ld

        def wsplit_res(src, dst_h, dst_l):
            i, slot, t_ld = split_common(src)
            t_cast = S.op("gpsimd", lambda q, d=dst_h, sl=slot: q.tensor_copy(
                d, sl), deps=[t_ld, p_last.get(("cast", i - 1))])
            p_last[("cast", i)] = t_cast
            t_sub = S.op("gpsimd", lambda q, d=dst_l, h=dst_h, sl=slot:
                         q.tensor_tensor(d, sl, h, op=ALU.subtract),
                         deps=[t_cast, p_last.get(("sub", i - 1))])
            p_last[("sub", i)] = t_sub
            return t_sub

        sp_pend = []

        def wsplit_stage2():
            i, slot, t_ld, dst_hl = sp_pend.pop(0)
            sbase = (i % 2) * 2048
            hs = wS[:, sbase:sbase + 1024]
            ls = wS[:, sbase + 1024:sbase + 2048]
            t_cast = S.op("scalar", lambda q, d=hs, sl=slot: q.activation(
                d, sl, AF.Identity), deps=[t_ld, p_last.get(("out", i - 2))])
            p_last[("cast", i)] = t_cast
            t_sub = S.op("gpsimd", lambda q, d=ls, h=hs, sl=slot:
                         q.tensor_tensor(d, sl, h, op=ALU.subtract),
                         deps=[t_cast, p_last.get(("sub", i - 1))])
            p_last[("sub", i)] = t_sub
            out = S.dma("gpsimd", "d_po%d" % (i % 2), dst_hl,
                        wS[:, sbase:sbase + 2048], deps=[t_sub])
            p_last[("out", i)] = out
            p_last[("done", i)] = out
            return i, out

        def wsplit_one(src, dst_hl):
            i, slot, t_ld = split_common(src)
            sp_pend.append((i, slot, t_ld, dst_hl))
            if len(sp_pend) > 2:
                return wsplit_stage2()
            return None

        split_jobs = []
        for e_ in range(E):
            for hh in range(2):
                for cg_ in range(G):
                    split_jobs.append((e_, up_v[e_][hh][:, cg_, :],
                                       sc_u_p[e_][hh][:, cg_, :, :]))
            for dg_ in range(G):
                split_jobs.append((e_, dn_v[e_][:, dg_, :],
                                   sc_d_p[e_][:, dg_, :, :]))
        sj_pos = [0]
        t_split_done = [[] for _ in range(E)]
        sp_expert = {}

        def pump_splits(n):
            for _ in range(n):
                if sj_pos[0] >= len(split_jobs):
                    while sp_pend:
                        fi, fout = wsplit_stage2()
                        t_split_done[sp_expert[fi]] = [fout]
                    return
                e_, src, dhl = split_jobs[sj_pos[0]]
                sj_pos[0] += 1
                r = wsplit_one(src, dhl)
                sp_expert[pc[0] - 1] = e_
                if r is not None:
                    fi, fout = r
                    t_split_done[sp_expert[fi]] = [fout]

        t_wsp_all = []
        for wi, wv in ((0, wiT_v), (1, wrT_v)):
            for cg_ in range(G):
                t_wsp_all.append(wsplit_res(
                    wv[:, cg_, :],
                    wBh_v[:, wi, cg_, :],
                    wBl_v[:, wi, cg_, :]))


        def ln_block(src_ap, dump, norm_dst, deps_in, war_deps,
                     extra_norm_deps):
            t_d = None
            for u in range(NT):
                t_d = S.op("scalar", lambda q, u=u: q.activation(
                    dump, src_ap[:, u * C:(u + 1) * C], AF.Square,
                    accum_out=bssq[:, u:u + 1]),
                    deps=(deps_in + war_deps if u == 0 else [t_d]))
                t_d = S.op("scalar", lambda q, u=u: q.activation(
                    dump, src_ap[:, u * C:(u + 1) * C], AF.Identity,
                    accum_out=bsum[:, u:u + 1]), deps=[t_d])
            t_sq = t_sums = t_ssq = t_d
            t_nm = S.op("vector", lambda q: q.tensor_scalar(
                bnm[:, :], bsum[:, :], -1.0 / C, None, op0=ALU.mult), deps=[t_sums])
            t_m2 = S.op("vector", lambda q: q.tensor_tensor(
                bm2t[:, :], bnm[:, :], bnm[:, :], op=ALU.mult), deps=[t_nm])
            t_var = S.op("vector", lambda q: q.scalar_tensor_tensor(
                bvar[:, :], bssq[:, :], 1.0 / C, bm2t[:, :],
                op0=ALU.mult, op1=ALU.subtract), deps=[t_ssq, t_m2])
            t_std = S.op("scalar", lambda q: q.activation(
                bstd[:, :], bvar[:, :], AF.Sqrt, bias=beps[:, 0:1]),
                deps=[t_var, t_eps])
            t_rstd = S.op("vector", lambda q: q.reciprocal(
                brstd[:, :], bstd[:, :]), deps=[t_std])
            t_norm = []
            for u in range(NT):
                t_norm.append(S.op("vector", lambda q, u=u, s=src_ap,
                                   d=norm_dst: q.tensor_scalar(
                                       d[:, u * C:(u + 1) * C],
                                       s[:, u * C:(u + 1) * C],
                                       bnm[:, u:u + 1], brstd[:, u:u + 1],
                                       op0=ALU.add, op1=ALU.mult),
                                   deps=[t_rstd] + list(extra_norm_deps)))
            return t_norm, t_d, t_var

        t_evac = {}
        t_s_psum = [None]

        def tmfm_split_transpose(src, t_src_ready, dst_war, want_f32,
                                 banks=None):
            t_cp = [[None] * 2 for _ in range(NT)]
            if banks is None:
                banks = (pb[0], pb[1])
            for u in range(NT):
                for hb in range(2):
                    bank = banks[(u * 2 + hb) % len(banks)]
                    for k in range(4):
                        g = hb * 4 + k
                        S.op("tensor", lambda q, u=u, g=g, k=k, bank=bank, s=src:
                             q.transpose(bank[:, k * 128:(k + 1) * 128],
                                         s[:, u * C + g * 128:u * C + (g + 1) * 128],
                                         bident[:, :]),
                             deps=([t_src_ready[u], t_id, t_evac.get(id(bank)),
                                    t_s_psum[0]]
                                   if k == 0 else []),
                             inc=(1 if k == 3 else 0))
                    t_tp = S.tok("tensor")
                    sl = (slice(None), slice(hb * 4, hb * 4 + 4),
                          slice(u * 128, (u + 1) * 128))
                    bank3 = bank.ap().rearrange("p (k t) -> p k t", t=128)
                    deps_c = [t_tp] + list(dst_war)
                    t_c32 = None
                    if want_f32:
                        t_c32 = S.op("vector", lambda q, d=bxf3[sl], b=bank3:
                                     q.tensor_copy(d, b), deps=deps_c)
                    t_ch = S.op("vector", lambda q, d=bxh3[sl], b=bank3:
                                q.tensor_copy(d, b), deps=deps_c)
                    if want_f32:
                        t_cl = S.op("vector", lambda q, d=bxl3[sl],
                                    s32=bxf3[sl], h=bxh3[sl]:
                                    q.tensor_tensor(
                                        d, s32, h, op=ALU.subtract),
                                    deps=[t_c32, t_ch])
                    else:
                        t_cl = S.op("vector", lambda q, d=bxl3[sl], b=bank3,
                                    h=bxh3[sl]: q.tensor_tensor(
                                        d, b, h, op=ALU.subtract),
                                    deps=[t_ch])
                    t_evac[id(bank)] = t_cl
                    t_cp[u][hb] = t_cl
            return t_cp

        t_bx_free = None
        t_bit_r_free = None
        t_bsplit_war = [t_rwl]
        t_bgx_r_free = None
        t_sp_at_prev = None
        t_sp_gx_prev = None
        t_carL = [None] * G
        t_eP_prev = None
        t_carP_all = None
        t_scr_free = None

        t_lx_pend = S.dma("sync", "d_x", bx3, x_v[0], deps=[t_bx_free])
        t_norm_pend, t_sqd_pend, t_var_prev = ln_block(
            bx.ap(), pstage.ap()[:, :], bx.ap(),
            deps_in=[t_lx_pend], war_deps=[t_rwl], extra_norm_deps=[])
        TB6 = [pb[0], pb[1], pb[2], pb[5], pb[6], pb[7]]
        t_ixg_last = None
        for q in range(CH):
            t_lx = t_lx_pend
            t_norm = t_norm_pend
            t_fmcp = tmfm_split_transpose(
                bx.ap(), t_norm, t_bsplit_war + [t_ixg_last], True, banks=TB6)
            t_fm_all = [t_fmcp[u][hb] for u in range(NT) for hb in range(2)]
            if q + 1 < CH:
                t_lx_pend = S.dma("sync", "d_x", bx3, x_v[q + 1],
                                  deps=[t_fm_all[-1]])

            t_sig = {0: [None] * G, 1: [None] * G}
            t_ixg = [None] * G
            t_red = [None] * G
            t_at = [None] * G
            t_ngg = [None] * G
            for wi in range(2):
                for dg in range(G):
                    bank = pb[3 + (dg % 2)]
                    passes = (((wBh, bxh), (wBh, bxl), (wBl, bxh))
                              if "no3p" not in _DBG else ((wBh, bxh),))
                    npss = len(passes) - 1
                    for cg in range(G):
                        for ps, (wb, xb) in enumerate(passes):
                            deps = []
                            if cg == 0 and ps == 0:
                                deps = t_fm_all + t_wsp_all + \
                                    [t_evac.get(id(bank))]
                            S.op("tensor", lambda q_, wi=wi, dg=dg, cg=cg,
                                 ps=ps, bank=bank, wb=wb, xb=xb:
                                 q_.matmul(
                                     bank[:, :tc],
                                     wb[:, wi * G * C + cg * C + dg * 128:
                                        wi * G * C + cg * C + (dg + 1) * 128],
                                     xb[:, cg * tc:(cg + 1) * tc],
                                     start=(cg == 0 and ps == 0),
                                     stop=(cg == G - 1 and ps == npss)),
                                 deps=deps,
                                 inc=(1 if (cg == G - 1 and ps == npss) else 0))
                    t_mm = S.tok("tensor")
                    if wi == 0:
                        t_s = S.op("scalar", lambda q_, dg=dg, bank=bank:
                                   q_.activation(
                                       bit[:, dg * tc:(dg + 1) * tc],
                                       bank[:, :tc], AF.Sigmoid,
                                       bias=bbi[:, dg:dg + 1]),
                                   deps=[t_mm, t_bi, t_bit_r_free])
                    else:
                        t_s = S.op("scalar", lambda q_, dg=dg, bank=bank:
                                   q_.activation(
                                       bat[:, dg * tc:(dg + 1) * tc],
                                       bank[:, :tc], AF.Sigmoid,
                                       bias=bbr[:, dg:dg + 1]),
                                   deps=[t_mm, t_br, t_sp_at_prev])
                    t_sig[wi][dg] = t_s
                    t_evac[id(bank)] = t_s
                    t_s_psum[0] = t_s
                    pump_splits(1)
                    if wi == 0:
                        t_ixg[dg] = S.op("vector", lambda q_, dg=dg:
                                         q_.tensor_tensor(
                                             bit[:, dg * tc:(dg + 1) * tc],
                                             bit[:, dg * tc:(dg + 1) * tc],
                                             bxf[:, dg * tc:(dg + 1) * tc],
                                             op=ALU.mult), deps=[t_s])
                    else:
                        t_red[dg] = S.op("vector", lambda q_, dg=dg:
                                         q_.reduce_sum(
                                             bsP[:, dg:dg + 1],
                                             bat[:, dg * tc:(dg + 1) * tc],
                                             axis=AX.X),
                                         deps=[t_s, t_eP_prev])
                        t_at[dg] = S.op("scalar", lambda q_, dg=dg:
                                        q_.activation(
                                            bat[:, dg * tc:(dg + 1) * tc],
                                            bat[:, dg * tc:(dg + 1) * tc],
                                            AF.Exp, scale=blam[:, 0:1]),
                                        deps=[t_s, t_c, t_red[dg]])
                        t_ngg[dg] = S.op("vector", lambda q_, dg=dg:
                                         q_.scalar_tensor_tensor(
                                             bgx[:, dg * tc:(dg + 1) * tc],
                                             bat[:, dg * tc:(dg + 1) * tc],
                                             -1.0,
                                             bat[:, dg * tc:(dg + 1) * tc],
                                             op0=ALU.mult, op1=ALU.mult),
                                         deps=[t_at[dg], t_sp_gx_prev,
                                               t_bgx_r_free])
            t_ixg_last = t_ixg[G - 1]
            t_pa_tok = S.tok("tensor")
            t_eP = S.op("scalar", lambda q_: q_.activation(
                btmpP[:, :], bsP[:, :], AF.Exp, scale=blam[:, 0:1]),
                deps=t_red + [t_carP_all])
            t_eP_prev = t_eP
            if q == 0:
                t_carP_all = S.op("vector", lambda q_: q_.tensor_copy(
                    bcarP[:, :], btmpP[:, :]), deps=[t_eP])
            else:
                t_carP_all = S.op("vector", lambda q_: q_.tensor_tensor(
                    bcarP[:, :], bcarP[:, :], btmpP[:, :], op=ALU.mult),
                    deps=[t_eP, t_carP_all])
            t_cfg = S.op("scalar", lambda q_: q_.activation(
                bgx[:, :], bgx[:, :], AF.Sqrt, bias=bone[:, 0:1]),
                deps=t_ngg + [t_one])
            pump_splits(7)
            if q + 1 < CH:
                t_norm_pend, t_sqd_pend, t_var_prev = ln_block(
                    bx.ap(), pstage.ap()[:, :], bx.ap(),
                    deps_in=[t_lx_pend, t_var_prev], war_deps=[],
                    extra_norm_deps=[])
            t_gxg = [None] * G
            for g in range(G):
                t_gxg[g] = S.op("vector", lambda q_, g=g: q_.tensor_tensor(
                    bgx[:, g * tc:(g + 1) * tc], bgx[:, g * tc:(g + 1) * tc],
                    bit[:, g * tc:(g + 1) * tc], op=ALU.mult),
                    deps=[t_cfg, t_ixg[g]])
                initL = 0.0 if q == 0 else bcarL[:, g:g + 1]
                t_ls = S.op("vector", lambda q_, g=g, initL=initL:
                            q_.tensor_tensor_scan(
                                bsilu[:, :], bat[:, g * tc:(g + 1) * tc],
                                bgx[:, g * tc:(g + 1) * tc], initL,
                                op0=ALU.mult, op1=ALU.add),
                            deps=[t_gxg[g], t_carL[g], t_scr_free])
                t_carL[g] = S.op("vector", lambda q_, g=g: q_.tensor_copy(
                    bcarL[:, g:g + 1], bsilu[:, tc - 1:tc]), deps=[t_ls])
                t_scr_free = t_carL[g]

            t_sp_at_prev = S.dma("sync", "d_spat", spat_v[q], bat3, deps=t_at)
            t_sp_gx_prev = S.dma("sync", "d_spgx", spgx_v[q], bgx3,
                                 deps=list(t_gxg))
            t_bit_r_free = t_gxg[G - 1]
            t_bsplit_war = [t_pa_tok]
            t_bgx_r_free = t_carL[G - 1]
            t_bx_free = t_fm_all[-1]

        t_e1 = S.dma("sync", "d_ag", agin.ap()[0], bcarP[:, :],
                     deps=[t_carP_all])
        t_e2 = S.dma("sync", "d_ag", agin.ap()[1], bcarL[:, :],
                     deps=[t_carL[g] for g in range(G)])
        t_cc = S.op("gpsimd", lambda q_: q_.collective_compute(
            "AllGather", ALU.bypass, replica_groups=[list(range(NCORES))],
            ins=[agin.ap().opt()], outs=[agout.ap().opt()]),
            deps=[t_e1, t_e2], sem="s_cc")
        t_bag = S.dma("sync", "d_bag",
                      bag.ap()[:, :NCORES * 2 * G].rearrange(
                          "p (r t g) -> p r t g", t=2, g=G),
                      agout_v, deps=[t_cc, t_bx_free])

        def cnd(i):
            return bag[:, bcand_o + i * G:bcand_o + (i + 1) * G]

        t_cand = [None] * NCORES
        t_cand[0] = S.op("vector", lambda q_: q_.memset(cnd(0), 0.0),
                         deps=[t_bag])
        t_cand[4] = S.op("vector", lambda q_: q_.memset(cnd(4), 0.0),
                         deps=[t_bag])
        t_btmp_r = None
        for b in range(2):
            for k in range(1, 4):
                i = 4 * b + k
                r = 4 * b + k - 1
                tm = S.op("vector", lambda q_, i=i, r=r: q_.tensor_tensor(
                    bag[:, btmp8_o:btmp8_o + G], bag[:, r * 16:r * 16 + G],
                    cnd(i - 1), op=ALU.mult),
                    deps=[t_bag, t_cand[i - 1], t_btmp_r])
                t_cand[i] = S.op("vector", lambda q_, i=i, r=r: q_.tensor_tensor(
                    cnd(i), bag[:, r * 16 + G:r * 16 + 2 * G],
                    bag[:, btmp8_o:btmp8_o + G], op=ALU.add), deps=[tm])
                t_btmp_r = t_cand[i]
        t_h = S.op("vector", lambda q_: q_.tensor_scalar(
            bag[:, bhp_o:bhp_o + G], cnd(0), bcsel[:, 0:1], None, op0=ALU.mult),
            deps=[t_cand[0], t_cs])
        for i in range(1, NCORES):
            src = ((i + 1) % 2) * G
            dst = (i % 2) * G
            t_h = S.op("vector", lambda q_, i=i, src=src, dst=dst:
                       q_.scalar_tensor_tensor(
                           bag[:, bhp_o + dst:bhp_o + dst + G], cnd(i),
                           bcsel[:, i:i + 1], bag[:, bhp_o + src:bhp_o + src + G],
                           op0=ALU.mult, op1=ALU.add),
                       deps=[t_cand[i], t_h])
        t_hrun = S.op("vector", lambda q_: q_.tensor_copy(
            bHrun[:, :], bag[:, bhp_o + G:bhp_o + 2 * G]), deps=[t_h])

        pump_splits(len(split_jobs))

        t_pa_pe = S.tok("tensor")
        t_lat_free = t_sp_at_prev
        t_lgx_free = t_sp_gx_prev
        t_bx_free_b = t_bx_free
        t_bit_free = t_bit_r_free
        t_ws2_war = [t_bit_r_free]
        t_hup = [t_hrun] * G
        t_out_prev = p_last.get(("sub", pc[0] - 1))
        t_up_used = {0: t_pa_pe, 1: t_pa_pe}
        t_dn_war = {0: [p_last.get(("out", pc[0] - 1)),
                        p_last.get(("out", pc[0] - 2))],
                    1: [t_bit_r_free]}
        t_hid_prev = t_scr_free
        t_bhid_war = None
        t_bhf_war = t_sqd_pend
        sp_at_all = ("d_spat", 16 * CH)
        sp_gx_all = ("d_spgx", 16 * CH)

        for q in range(CH):
            t_lat = S.dma("scalar", "d_lat", bat3, spat_v[q],
                          deps=[sp_at_all, t_lat_free])
            t_lgx = S.dma("scalar", "d_lgx", bgx3, spgx_v[q],
                          deps=[sp_gx_all, t_lgx_free])
            t_lx = S.dma("sync", "d_x", bx3, x_v[q],
                         deps=[t_bx_free_b, t_hrun if q == 0 else None])

            t_scan = [None] * G
            for g in range(G):
                t_scan[g] = S.op("vector", lambda q_, g=g: q_.tensor_tensor_scan(
                    bit[:, g * tc:(g + 1) * tc], bat[:, g * tc:(g + 1) * tc],
                    bgx[:, g * tc:(g + 1) * tc], bHrun[:, g:g + 1],
                    op0=ALU.mult, op1=ALU.add),
                    deps=[t_lat, t_lgx, t_hup[g], t_bit_free])
                t_hup[g] = S.op("vector", lambda q_, g=g: q_.tensor_copy(
                    bHrun[:, g:g + 1], bit[:, g * tc + tc - 1:g * tc + tc]),
                    deps=[t_scan[g]])
            t_lat_free = t_scan[G - 1]
            t_lgx_free = t_scan[G - 1]

            t_res = []
            for u in range(NT):
                for hb in range(2):
                    bank = pb[hb]
                    for k in range(4):
                        g = hb * 4 + k
                        S.op("tensor", lambda q_, u=u, g=g, k=k, bank=bank:
                             q_.transpose(bank[:, k * 128:(k + 1) * 128],
                                          bit[:, g * tc + u * 128:g * tc + (u + 1) * 128],
                                          bident[:, :]),
                             deps=([t_scan[hb * 4 + kk] for kk in range(4)]
                                   + [t_id, t_evac.get(id(bank)), t_s_psum[0]]
                                   if k == 0 else []),
                             inc=(1 if k == 3 else 0))
                    t_tp = S.tok("tensor")
                    t_r = S.op("vector", lambda q_, u=u, hb=hb, bank=bank:
                               q_.tensor_tensor(
                                   bacc[:, u * C + hb * 512:u * C + (hb + 1) * 512],
                                   bank[:, :512],
                                   bx[:, u * C + hb * 512:u * C + (hb + 1) * 512],
                                   op=ALU.add),
                               deps=[t_tp, t_lx, t_out_prev])
                    t_evac[id(bank)] = t_r
                    t_res.append(t_r)
            t_bx_free_b = t_res[-1]

            t_norm, t_sqd2, t_var_prev = ln_block(
                bacc.ap(), bgx.ap()[:, :C], bit.ap(),
                deps_in=t_res + [t_var_prev], war_deps=[t_scan[G - 1]],
                extra_norm_deps=[t_hup[G - 1]])
            t_lgx_free = t_sqd2

            t_x2f = tmfm_split_transpose(bit.ap(), t_norm, t_bsplit_war, False)
            t_x2f_all = [t_x2f[u][hb] for u in range(NT) for hb in range(2)]

            t_gate = [None] * NT
            prbank = pb[2]
            for u in range(NT if "norouter" not in _DBG else 0):
                for cg in range(G):
                    for ps, (xb, rb) in enumerate(
                            ((bxh, brwh), (bxl, brwh), (bxh, brwl))):
                        deps = []
                        if cg == 0 and ps == 0:
                            deps = [t_x2f[u][0], t_x2f[u][1], t_rwl,
                                    t_evac.get(id(prbank))]
                        S.op("tensor", lambda q_, u=u, cg=cg, ps=ps, xb=xb, rb=rb:
                             q_.matmul(
                                 prbank[:, :E],
                                 xb[:, cg * tc + u * 128:cg * tc + (u + 1) * 128],
                                 rb[:, cg * E:(cg + 1) * E],
                                 start=(cg == 0 and ps == 0),
                                 stop=(cg == G - 1 and ps == 2)),
                             deps=deps,
                             inc=(1 if (cg == G - 1 and ps == 2) else 0))
                t_rmm = S.tok("tensor")
                t_m1 = S.op("vector", lambda q_, u=u: q_.reduce_max(
                    bnm1[:, u:u + 1], prbank[:, :E], axis=AX.X, negate=True),
                    deps=[t_rmm])
                t_ex = S.op("scalar", lambda q_, u=u: q_.activation(
                    bge[:, u * E:(u + 1) * E], prbank[:, :E], AF.Exp,
                    bias=bnm1[:, u:u + 1]), deps=[t_m1, t_x2f_all[-1]])
                t_s_psum[0] = t_ex
                t_mk = S.op("vector", lambda q_, u=u: q_.tensor_scalar(
                    bmsk[:, u * E:(u + 1) * E], prbank[:, :E],
                    bnm1[:, u:u + 1], 0.0, op0=ALU.add, op1=ALU.is_ge),
                    deps=[t_m1])
                t_mk2 = S.op("vector", lambda q_, u=u: q_.scalar_tensor_tensor(
                    bmsk[:, u * E:(u + 1) * E], bmsk[:, u * E:(u + 1) * E],
                    -BIG, prbank[:, :E], op0=ALU.mult, op1=ALU.add),
                    deps=[t_mk])
                t_m2r = S.op("vector", lambda q_, u=u: q_.reduce_max(
                    bnm2[:, u:u + 1], bmsk[:, u * E:(u + 1) * E], axis=AX.X,
                    negate=True), deps=[t_mk2])
                t_se = S.op("vector", lambda q_, u=u: q_.tensor_scalar(
                    bsel[:, u * E:(u + 1) * E], prbank[:, :E],
                    bnm2[:, u:u + 1], 0.0, op0=ALU.add, op1=ALU.is_ge),
                    deps=[t_m2r])
                t_g0 = S.op("vector", lambda q_, u=u: q_.tensor_tensor(
                    bge[:, u * E:(u + 1) * E], bge[:, u * E:(u + 1) * E],
                    bsel[:, u * E:(u + 1) * E], op=ALU.mult), deps=[t_se, t_ex])
                t_evac[id(prbank)] = t_g0
                t_gs = S.op("vector", lambda q_, u=u: q_.reduce_sum(
                    bgs[:, u:u + 1], bge[:, u * E:(u + 1) * E], axis=AX.X),
                    deps=[t_g0])
                t_gr = S.op("vector", lambda q_, u=u: q_.reciprocal(
                    bgr[:, u:u + 1], bgs[:, u:u + 1]), deps=[t_gs])
                t_gate[u] = S.op("vector", lambda q_, u=u: q_.tensor_scalar(
                    bgate[:, u * E:(u + 1) * E], bge[:, u * E:(u + 1) * E],
                    bgr[:, u:u + 1], None, op0=ALU.mult), deps=[t_gr])

            t_acc_last = t_res
            for e in range(E if "noffn" not in _DBG else 0):
                sd = t_split_done[e]
                sc_u_e = sc_u[e].rearrange("p (pp hl d1) -> p hl pp d1", hl=2, d1=1024)
                t_upA = S.dma("sync", "d_uph",
                              wBh.ap().rearrange("p (pp d1) -> p pp d1", d1=C),
                              sc_u_e[:, 0], deps=[t_up_used[0]] + sd)
                t_upB = S.dma("sync", "d_upl",
                              wBl.ap().rearrange("p (pp d1) -> p pp d1", d1=C),
                              sc_u_e[:, 1], deps=[t_up_used[1]] + sd)
                t_hid = [None] * G
                for dk in range(G):
                    for which, bank in ((0, pb[3]), (1, pb[4])):
                        for cg in range(G):
                            for ps, (wb, xb) in enumerate(
                                    ((wBh, bxh), (wBh, bxl), (wBl, bxh))):
                                deps = []
                                if cg == 0 and ps == 0:
                                    deps = t_x2f_all + [t_upA, t_upB,
                                                       t_evac.get(id(bank))]
                                S.op("tensor", lambda q_, dk=dk, cg=cg, ps=ps,
                                     which=which, bank=bank, wb=wb, xb=xb:
                                     q_.matmul(
                                         bank[:, :512],
                                         wb[:, which * G * C + cg * C + dk * 128:
                                            which * G * C + cg * C + (dk + 1) * 128],
                                         xb[:, cg * tc:(cg + 1) * tc],
                                         start=(cg == 0 and ps == 0),
                                         stop=(cg == G - 1 and ps == 2)),
                                     deps=deps,
                                     inc=(1 if (cg == G - 1 and ps == 2) else 0))
                        if which == 0:
                            t_h1 = S.tok("tensor")
                        else:
                            t_h2 = S.tok("tensor")
                    t_si = S.op("scalar", lambda q_: q_.activation(
                        bsilu[:, :], pb[3][:, :512], AF.Silu),
                        deps=[t_h1, t_hid_prev])
                    t_evac[id(pb[3])] = t_si
                    t_s_psum[0] = t_si
                    t_tt = S.op("vector", lambda q_: q_.tensor_tensor(
                        bhf[:, :512], bsilu[:, :], pb[4][:, :512], op=ALU.mult),
                        deps=[t_h2, t_si, t_bhf_war, t_scan[G - 1]])
                    t_evac[id(pb[4])] = t_tt
                    t_hh2 = S.op("scalar", lambda q_, dk=dk: q_.activation(
                        bhh[:, dk * 512:(dk + 1) * 512], bhf[:, :512],
                        AF.Identity), deps=[t_tt, t_bhid_war])
                    t_hl2 = S.op("vector", lambda q_, dk=dk: q_.tensor_tensor(
                        bhl[:, dk * 512:(dk + 1) * 512], bhf[:, :512],
                        bhh[:, dk * 512:(dk + 1) * 512], op=ALU.subtract),
                        deps=[t_hh2])
                    t_bhf_war = t_hl2
                    t_hid[dk] = t_hl2
                    t_hid_prev = t_tt
                    t_bhf_war = t_hl2

                t_up_done = S.tok("tensor")

                t_dmm = {}
                t_dld = {}
                for r in range(4):
                    rb = wS.ap() if r % 2 == 0 else wS2[:, :4096]
                    t_dh = S.dma("sync", "d_dnh",
                                 rb[:, :2048].rearrange(
                                     "p (k d) -> p k d", k=2),
                                 sc_d_p[e][:, 2 * r:2 * r + 2, 0, :],
                                 deps=t_dn_war[r % 2])
                    t_dl = S.dma("sync", "d_dnl",
                                 rb[:, 2048:4096].rearrange(
                                     "p (k d) -> p k d", k=2),
                                 sc_d_p[e][:, 2 * r:2 * r + 2, 1, :],
                                 deps=t_dn_war[r % 2])
                    t_dld[r] = [t_dh, t_dl]
                    for u in range(NT):
                        for hc in range(2):
                            bank = pb[u * 2 + hc]
                            for dgl in range(2):
                                dg = 2 * r + dgl
                                for ps, (hb_, dof) in enumerate(
                                        ((bhh, 0), (bhh, 2048), (bhl, 0))):
                                    deps = []
                                    if r == 0 and dgl == 0 and ps == 0:
                                        deps = [t_dh, t_dl,
                                                t_evac.get(id(bank))] + t_hid
                                    elif dgl == 0 and ps == 0 and u == 0 and hc == 0:
                                        deps = [t_dh, t_dl]
                                    S.op("tensor", lambda q_, u=u, hc=hc, dg=dg,
                                         dgl=dgl, r=r, ps=ps, bank=bank,
                                         hb_=hb_, dof=dof, rb=rb:
                                         q_.matmul(
                                             bank[:, :512],
                                             hb_[:, dg * 512 + u * 128:
                                                 dg * 512 + (u + 1) * 128],
                                             rb[:, dof + dgl * C + hc * 512:
                                                dof + dgl * C + (hc + 1) * 512],
                                             start=(r == 0 and dgl == 0 and ps == 0),
                                             stop=(r == 3 and dgl == 1 and ps == 2)),
                                         deps=deps,
                                         inc=(1 if (dgl == 1 and ps == 2
                                                    and ((u == NT - 1 and hc == 1)
                                                         or r == 3))
                                              else 0))
                                if r == 3 and dgl == 1:
                                    t_dmm[(u, hc)] = S.tok("tensor")
                    t_dn_war[r % 2] = [S.tok("tensor")]
                t_acc_new = [None] * 8
                for u, hc in ((0, 0), (0, 1), (1, 0), (1, 1), (2, 0), (2, 1),
                              (3, 0), (3, 1)):
                    bank = pb[u * 2 + hc]
                    t_a = S.op("vector", lambda q_, u=u, hc=hc, e=e,
                               bank=bank: q_.scalar_tensor_tensor(
                                   bacc[:, u * C + hc * 512:u * C + (hc + 1) * 512],
                                   bank[:, :512],
                                   bgate[:, u * E + e:u * E + e + 1],
                                   bacc[:, u * C + hc * 512:u * C + (hc + 1) * 512],
                                   op0=ALU.mult, op1=ALU.add),
                               deps=[t_dmm[(u, hc)], t_gate[u],
                                     t_acc_last[u * 2 + hc]])
                    t_evac[id(bank)] = t_a
                    t_acc_new[u * 2 + hc] = t_a
                t_acc_last = t_acc_new
                t_pe_now = S.tok("tensor")
                t_up_used = {0: t_up_done, 1: t_up_done}
                t_bhid_war = t_pe_now
                t_bsplit_war = [t_pe_now]

            t_out_prev = S.dma("sync", "d_out", out_v[q], bacc3,
                               deps=t_acc_last)
            t_lat_free = t_bhf_war
            t_bit_free = t_x2f_all[-1]

        S.fence("sync", [("d_out", 16 * CH), t_out_prev])
        S.build(block)
    return nc


def _stage(inputs, tok):
    x = np.ascontiguousarray(np.asarray(inputs["L0_attn_x"], np.float32)
                             .reshape(-1, C))
    wiT = np.ascontiguousarray(np.asarray(inputs["L0_attn_Wi"], np.float32).T)
    wrT = np.ascontiguousarray(np.asarray(inputs["L0_attn_Wr"], np.float32).T)
    bi = np.ascontiguousarray(np.asarray(inputs["L0_attn_bi"], np.float32)
                              .reshape(G, 128).T)
    br = np.ascontiguousarray(np.asarray(inputs["L0_attn_br"], np.float32)
                              .reshape(G, 128).T)
    lam = np.ascontiguousarray(
        np.tile(np.asarray(inputs["L0_attn_cap_lambda"], np.float32)
                .reshape(1, 1), (128, 1)))
    rw = np.ascontiguousarray(np.asarray(inputs["L0_ffn_router_w"], np.float32).T)
    up = np.ascontiguousarray(np.asarray(inputs["L0_ffn_up_proj"], np.float32))
    dn = np.ascontiguousarray(np.asarray(inputs["L0_ffn_down_proj"], np.float32))
    in_maps = []
    for c in range(NCORES):
        csel = np.ascontiguousarray(
            np.tile(np.eye(NCORES, dtype=np.float32)[c], (128, 1)))
        in_maps.append({
            "x": np.ascontiguousarray(x[c * tok:(c + 1) * tok]),
            "wiT": wiT, "wrT": wrT, "bi": bi, "br": br, "lam": lam,
            "rw": rw, "up": up, "dn": dn, "csel": csel,
        })
    return in_maps


def kernel(**inputs):
    B, T, _ = inputs["L0_attn_x"].shape
    tok = B * T // NCORES
    in_maps = _stage(inputs, tok)
    nc = build_program(tok=tok, tc=512)
    res = run_bass_kernel_spmd(nc, in_maps, core_ids=list(range(NCORES)))
    out = np.concatenate([np.asarray(r["out"]) for r in res.results], axis=0)
    return out.reshape(B, T, C).astype(np.float32)
